# version 15
# baseline (speedup 1.0000x reference)
import numpy as np
import ml_dtypes
from contextlib import ExitStack

import concourse.bass as bass
import concourse.mybir as mybir
from concourse.bass_utils import run_bass_kernel_spmd
from concourse.aot_env import allow_ici_all_to_all

F32 = mybir.dt.float32
BF16 = mybir.dt.bfloat16
F16 = mybir.dt.float16
I8 = mybir.dt.int8
I32 = mybir.dt.int32
AF = mybir.ActivationFunctionType
ALU = mybir.AluOpType
AX = mybir.AxisListType

B, T, C, D = 4, 2048, 2048, 128
W = 8
TOK = B * T
TOWN = TOK // W
NT = TOWN // 128
CCH = C // 128
SCALE_S = float(1.0 / np.sqrt(D))
THETA = 0.4999999701976776
NEGBIG = -1.0e30
PHASES = 99
P1_QUANT = True
P1_TP = True
DEBUG_DUMP = False


def build():
    nc = bass.Bass("TRN2", target_bir_lowering=False, detect_race_conditions=False)

    x_own = nc.dram_tensor("x_own", [TOWN, C], F32, kind="ExternalInput").ap()
    wqk = nc.dram_tensor("wqk", [32, 128, CCH, 128], F32, kind="ExternalInput").ap()
    bqk = nc.dram_tensor("bqk", [128, 32], F32, kind="ExternalInput").ap()
    wv = nc.dram_tensor("wv", [4, 128, CCH, 512], F32, kind="ExternalInput").ap()
    vb_rep = nc.dram_tensor("vb_rep", [128, 2048], F32, kind="ExternalInput").ap()
    wffn = nc.dram_tensor("wffn", [CCH, 128, 2048], I8, kind="ExternalInput").ap()
    wsc_rep = nc.dram_tensor("wsc_rep", [128, 2048], F32, kind="ExternalInput").ap()
    fb_rep = nc.dram_tensor("fb_rep", [128, 2048], F32, kind="ExternalInput").ap()
    wout = nc.dram_tensor("wout", [CCH, 128, 2048], F32, kind="ExternalInput").ap()
    outb_rep = nc.dram_tensor("outb_rep", [128, 2048], F32, kind="ExternalInput").ap()
    embt = nc.dram_tensor("embt", [128, TOWN], F32, kind="ExternalInput").ap()
    maskb = nc.dram_tensor("maskb", [128, 4, 512], F32, kind="ExternalInput").ap()
    identf = nc.dram_tensor("identf", [128, 128], F32, kind="ExternalInput").ap()
    identb = nc.dram_tensor("identb", [128, 128], BF16, kind="ExternalInput").ap()
    onesb = nc.dram_tensor("onesb", [128, 128], BF16, kind="ExternalInput").ap()
    kconst = nc.dram_tensor("kconst", [128, 2], F32, kind="ExternalInput").ap()

    y = nc.dram_tensor("y", [TOWN, C], F32, kind="ExternalOutput").ap()

    ffn16d = nc.dram_tensor("ffn16d", [TOWN, C], F16)
    PAD = [4 * 1024 * 1024]
    nc.dram_tensor("pad_a", PAD, F32)
    aqk_i = nc.dram_tensor("aqk_i", [4096, TOWN], BF16)
    nc.dram_tensor("pad_b", PAD, F32)
    aqk_o = nc.dram_tensor("aqk_o", [4096, TOWN], BF16)
    nc.dram_tensor("pad_c", PAD, F32)
    av_i = nc.dram_tensor("av_i", [TOK, 256], BF16)
    nc.dram_tensor("pad_d", PAD, F32)
    av_o = nc.dram_tensor("av_o", [TOK, 256], BF16)
    nc.dram_tensor("pad_e", PAD, F32)
    aav_i = nc.dram_tensor("aav_i", [2048, TOWN], BF16)
    nc.dram_tensor("pad_f", PAD, F32)
    aav_o = nc.dram_tensor("aav_o", [2048, TOWN], BF16)
    nc.dram_tensor("pad_g", PAD, F32)
    bar1_i = nc.dram_tensor("bar1_i", [16, 64], F32)
    nc.dram_tensor("pad_h", PAD, F32)
    bar1_o = nc.dram_tensor("bar1_o", [128, 64], F32)
    nc.dram_tensor("pad_i", PAD, F32)
    bar2_i = nc.dram_tensor("bar2_i", [16, 64], F32)
    nc.dram_tensor("pad_j", PAD, F32)
    bar2_o = nc.dram_tensor("bar2_o", [128, 64], F32)
    nc.dram_tensor("pad_k", PAD, F32)
    dbg = {}
    if DEBUG_DUMP:
        for nm, shape, dt in [("d_aqk_i", [4096, TOWN], BF16), ("d_aqk_o", [4096, TOWN], BF16),
                              ("d_av_i", [TOK, 256], BF16), ("d_av_o", [TOK, 256], BF16),
                              ("d_aav_i", [2048, TOWN], BF16), ("d_aav_o", [2048, TOWN], BF16),
                              ("d_ffn16", [TOWN, C], F16),
                              ("d_xnT", [CCH, 128, TOWN], BF16),
                              ("d_qT", [CCH, 128, TOWN], BF16),
                              ("d_qst", [128, 2048], F32),
                              ("d_vq", [128, 2048], F32),
                              ("d_tq", [128, 2048], I32),
                              ("d_dq", [128, 2048], F32),
                              ("d_xn7", [128, 2048], F32),
                              ("d_aqk_pre", [4096, TOWN], BF16),
                              ("d_ffn16b", [TOWN, C], F16),
                              ("d_aqk_ob", [4096, TOWN], BF16),
                              ("d_nrm", [128, 12], F32),
                              ("d_strc", [128, 2, 512], F32),
                              ("d_stav", [128, 2, 512], BF16),
                              ("d_pt", [128, 4, 512], BF16),
                              ("d_aav_pre", [2048, TOWN], BF16),
                              ("d_aav_p1", [2048, TOWN], BF16),
                              ("d_aav_p3", [2048, TOWN], BF16)]:
            dbg[nm] = nc.dram_tensor(nm, shape, dt, kind="ExternalOutput").ap()

    es = ExitStack()
    sb = lambda name, shape, dt: es.enter_context(nc.sbuf_tensor(name, shape, dt))

    embt_s = sb("embt_s", [128, TOWN], F32)
    bqk_s = sb("bqk_s", [128, 32], F32)
    maskb_s = sb("maskb_s", [128, 4, 512], F32)
    identf_s = sb("identf_s", [128, 128], F32)
    identb_s = sb("identb_s", [128, 128], BF16)
    onesb_s = sb("onesb_s", [128, 128], BF16)
    insc = sb("insc", [128, NT], F32)
    rep0 = sb("rep0", [128, 2048], F32)
    rep1 = sb("rep1", [128, 2048], F32)
    qT_ffn = sb("qT_ffn", [128, CCH, TOWN], BF16)
    st_qk = sb("st_qk", [128, 2, 512], BF16)
    st_v = sb("st_v", [128, 2, 512], BF16)
    nrm = sb("nrm", [128, 12], F32)
    ln1 = sb("ln1", [128, 16], F32)
    kconst_s = sb("kconst_s", [128, 2], F32)
    ev_tmp = sb("ev_tmp", [128, 512], F32)
    settle_s = sb("settle_s", [128, 32], BF16)

    pb = [es.enter_context(nc.psum_tensor(f"pb{i}", [128, 512], F32)) for i in range(8)]

    cnt = {}
    sems = {}
    for name in ["sLD", "sGW", "sST", "sV", "sS", "sT", "sCC"]:
        sems[name] = es.enter_context(nc.semaphore(name))
        cnt[name] = 0

    def inc(ins, sem, k=1):
        cnt[sem] += k
        return ins.then_inc(sems[sem], k)

    def snap(sem):
        return (sem, cnt[sem])

    def wt(eng, rel):
        if rel is not None and rel[1] > 0:
            eng.wait_ge(sems[rel[0]], rel[1])

    def sw(eng, ins, sem):
        inc(ins, sem)
        eng.wait_ge(sems[sem], cnt[sem])

    with es, nc.Block() as blk:
        def p0(sync):
            inc(sync.dma_start(out=embt_s[:], in_=embt), "sLD", 16)
            inc(sync.dma_start(out=bqk_s[:], in_=bqk), "sLD", 16)
            inc(sync.dma_start(out=maskb_s[:], in_=maskb), "sLD", 16)
            inc(sync.dma_start(out=identf_s[:], in_=identf), "sLD", 16)
            inc(sync.dma_start(out=identb_s[:], in_=identb), "sLD", 16)
            inc(sync.dma_start(out=onesb_s[:], in_=onesb), "sLD", 16)
            inc(sync.dma_start(out=rep0[:], in_=vb_rep), "sLD", 16)
            inc(sync.dma_start(out=kconst_s[:], in_=kconst), "sLD", 16)
        blk.sync(p0)
        consts_ok = snap("sLD")

        es1 = ExitStack()
        sb1 = lambda n, s, d: es1.enter_context(nc.sbuf_tensor(n, s, d))
        xst = sb1("xst", [128, 2, 2048], F32)
        xnst = sb1("xnst", [128, 2, 2048], F32)
        qst = sb1("qst", [128, 2048], F32)
        xnT = sb1("xnT", [128, CCH, TOWN], BF16)
        wqk_b = sb1("wqk_b", [128, 2, CCH, 128], BF16)
        wv_b = sb1("wv_b", [128, 2, CCH, 512], BF16)
        es1a = ExitStack()
        sb1a = lambda n, s, d: es1a.enter_context(nc.sbuf_tensor(n, s, d))
        vq_s = sb1a("vq_s", [128, 2048], F32)
        tq_s = sb1a("tq_s", [128, 2048], I32)
        dq_s = sb1a("dq_s", [128, 2048], F32)

        x_rel = [None, None]
        xn_rel = [None, None]
        tslot_rel = [None, None]

        for i in range(NT):
            s = i % 2
            o = s * 8
            qst_rel = snap("sT")

            def p1_load(sync, i=i, s=s):
                wt(sync, x_rel[s])
                inc(sync.dma_start(out=xst[:, s, :],
                                   in_=x_own[i * 128:(i + 1) * 128, :]), "sLD", 16)
            blk.sync(p1_load)
            x_ready = snap("sLD")

            def p1_dve1(vector, s=s, o=o, x_ready=x_ready):
                wt(vector, x_ready)
                sw(vector, vector.reduce_sum(ln1[:, o + 0:o + 1], xst[:, s, :],
                                             axis=AX.X), "sV")
                inc(vector.tensor_scalar_mul(ln1[:, o + 1:o + 2], ln1[:, o + 0:o + 1],
                                             -1.0 / 2048.0), "sV")
            blk.vector(p1_dve1)
            negm_ok = snap("sV")

            def p1_sc1(scalar, s=s, o=o, negm_ok=negm_ok):
                wt(scalar, negm_ok)
                sw(scalar, scalar.activation(dq_s[:], xst[:, s, :], AF.Square,
                                  bias=ln1[:, o + 1:o + 2],
                                  accum_out=ln1[:, o + 2:o + 3]), "sS")
                inc(scalar.activation(ln1[:, o + 3:o + 4], ln1[:, o + 2:o + 3], AF.Sqrt,
                                      bias=kconst_s[:, 0:1], scale=1.0 / 2048.0), "sS")
            blk.scalar(p1_sc1)
            sd_ok = snap("sS")

            def p1_dve2(vector, i=i, s=s, o=o, sd_ok=sd_ok, qst_rel=qst_rel):
                wt(vector, sd_ok)
                sw(vector, vector.reciprocal(ln1[:, o + 4:o + 5], ln1[:, o + 3:o + 4]),
                   "sV")
                wt(vector, xn_rel[s])
                inc(vector.tensor_scalar(xnst[:, s, :], xst[:, s, :], ln1[:, o + 1:o + 2],
                                         ln1[:, o + 4:o + 5], ALU.add, ALU.mult), "sV")
                sw(vector, vector.reduce_max(ln1[:, o + 5:o + 6], xnst[:, s, :],
                                             axis=AX.X, apply_absolute_value=True), "sV")
                sw(vector, vector.tensor_scalar_max(ln1[:, o + 6:o + 7],
                                                    ln1[:, o + 5:o + 6], 1e-7), "sV")
                vector.tensor_scalar_mul(insc[:, i:i + 1], ln1[:, o + 6:o + 7], 1.0 / 127.0)
                if not P1_QUANT:
                    inc(vector.tensor_copy(qst[:], xnst[:, s, :]), "sV")
                    return
                sw(vector, vector.reciprocal(ln1[:, o + 7:o + 8], ln1[:, o + 6:o + 7]),
                   "sV")
                sw(vector, vector.tensor_scalar_mul(ln1[:, o + 7:o + 8],
                                                    ln1[:, o + 7:o + 8], -127.0), "sV")
                vector.tensor_scalar_mul(vq_s[:], xnst[:, s, :], ln1[:, o + 7:o + 8])
                vector.tensor_copy(tq_s[:], vq_s[:])
                vector.tensor_sub(dq_s[:], vq_s[:], tq_s[:])
                vector.scalar_tensor_tensor(vq_s[:], dq_s[:], THETA, tq_s[:],
                                            ALU.is_gt, ALU.add)
                wt(vector, qst_rel)
                inc(vector.scalar_tensor_tensor(qst[:], dq_s[:], -THETA, vq_s[:],
                                                ALU.is_lt, ALU.subtract), "sV")
            blk.vector(p1_dve2)
            xn_ok = snap("sV")

            vbase = cnt["sV"]
            tslot_prev = list(tslot_rel)
            t0 = cnt["sT"]
            if not P1_TP:
                x_rel[s] = xn_ok
                xn_rel[s] = xn_ok
                continue

            def p1_pe(tensor, s=s, vbase=vbase, tslot_prev=tslot_prev, xn_ok=xn_ok):
                wt(tensor, consts_ok)
                wt(tensor, xn_ok)
                for g in range(8):
                    src = xnst[:, s, :] if g < 4 else qst[:]
                    jb = (g % 4) * 4
                    bank = pb[6 + g % 2]
                    if g < 2:
                        wt(tensor, tslot_prev[g % 2])
                    else:
                        tensor.wait_ge(sems["sV"], vbase + g - 1)
                    for k in range(4):
                        ins = tensor.transpose(bank[:, k * 128:(k + 1) * 128],
                                               src[:, (jb + k) * 128:(jb + k + 1) * 128],
                                               identf_s[:])
                    inc(ins, "sT")
            blk.tensor(p1_pe)

            def p1_evac(vector, i=i, t0=t0):
                for g in range(8):
                    jb = (g % 4) * 4
                    bank = pb[6 + g % 2].ap().rearrange("p (a b) -> p a b", a=4)
                    dst = xnT if g < 4 else qT_ffn
                    vector.wait_ge(sems["sT"], t0 + g + 1)
                    inc(vector.tensor_copy(dst[:, jb:jb + 4, i * 128:(i + 1) * 128],
                                           bank), "sV")
            blk.vector(p1_evac)
            tslot_rel = [("sV", vbase + 7), ("sV", vbase + 8)]
            x_rel[s] = xn_ok
            xn_rel[s] = snap("sV")

        p1_done_v = snap("sV")
        if DEBUG_DUMP:
            def p1_dump(sync):
                wt(sync, p1_done_v)
                inc(sync.dma_start(out=dbg["d_qst"], in_=qst[:]), "sST", 16)
                inc(sync.dma_start(out=dbg["d_vq"], in_=vq_s[:]), "sST", 16)
                inc(sync.dma_start(out=dbg["d_tq"], in_=tq_s[:]), "sST", 16)
                inc(sync.dma_start(out=dbg["d_dq"], in_=dq_s[:]), "sST", 16)
                inc(sync.dma_start(out=dbg["d_xn7"], in_=xnst[:, 1, :]), "sST", 16)
                inc(sync.dma_start(out=dbg["d_xnT"].rearrange("a p b -> p a b"),
                                   in_=xnT[:]), "sST", 16)
                inc(sync.dma_start(out=dbg["d_qT"].rearrange("a p b -> p a b"),
                                   in_=qT_ffn[:]), "sST", 16)
            blk.sync(p1_dump)
            def p1_dump_wait(sync):
                sync.wait_ge(sems["sST"], cnt["sST"])
            blk.sync(p1_dump_wait)
        es1a.close()
        if PHASES < 2:
            es1.close()
            def fin0(sync):
                sync.wait_ge(sems["sV"], cnt["sV"])
            blk.sync(fin0)
            return nc

        T0 = cnt["sT"]
        T0v = T0 + 64
        gw_base = cnt["sGW"]

        def p2_w(gpsimd):
            for fc in range(32):
                if fc >= 2:
                    gpsimd.wait_ge(sems["sT"], T0 + 2 * (fc - 1))
                inc(gpsimd.dma_start(out=wqk_b[:, fc % 2, :, :], in_=wqk[fc]), "sGW", 16)
                inc(gpsimd.dma_start(out=settle_s[:, 0:16],
                                     in_=wqk_b[:, fc % 2, CCH - 1, 112:128]), "sGW", 16)
            for rp in range(4):
                if rp >= 2:
                    gpsimd.wait_ge(sems["sT"], T0v + (rp - 1) * 8)
                inc(gpsimd.dma_start(out=wv_b[:, rp % 2, :, :], in_=wv[rp]), "sGW", 16)
                inc(gpsimd.dma_start(out=settle_s[:, 16:32],
                                     in_=wv_b[:, rp % 2, CCH - 1, 496:512]), "sGW", 16)
        blk.gpsimd(p2_w)

        pqk_rel = [None, None]
        stqk_rel = [None, None]
        for fc in range(32):
            def p2_mm(tensor, fc=fc):
                if fc == 0:
                    wt(tensor, p1_done_v)
                tensor.wait_ge(sems["sGW"], gw_base + (fc + 1) * 32)
                for th in range(2):
                    wt(tensor, pqk_rel[th])
                    for cc in range(CCH):
                        ins = tensor.matmul(pb[th][:], wqk_b[:, fc % 2, cc, :],
                                            xnT[:, cc, th * 512:(th + 1) * 512],
                                            start=(cc == 0), stop=(cc == CCH - 1))
                    inc(ins, "sT")
            blk.tensor(p2_mm)
            mm_now = cnt["sT"]
            assert mm_now == T0 + 2 * (fc + 1)

            def p2_ev(vector, fc=fc, mm_now=mm_now):
                if fc == 0:
                    wt(vector, consts_ok)
                for th in range(2):
                    vector.wait_ge(sems["sT"], mm_now - 1 + th)
                    wt(vector, stqk_rel[th])
                    vector.tensor_scalar_add(ev_tmp[:], pb[th][:],
                                             bqk_s[:, fc:fc + 1])
                    inc(vector.tensor_tensor(
                        st_qk[:, th, :], ev_tmp[:],
                        embt_s[:, th * 512:(th + 1) * 512], ALU.mult), "sV")
            blk.vector(p2_ev)
            ev_now = cnt["sV"]
            pqk_rel = [("sV", ev_now - 1), ("sV", ev_now)]

            def p2_st(sync, fc=fc, ev_now=ev_now):
                for th in range(2):
                    sync.wait_ge(sems["sV"], ev_now - 1 + th)
                    inc(sync.dma_start(
                        out=aqk_i.ap()[fc * 128:(fc + 1) * 128, th * 512:(th + 1) * 512],
                        in_=st_qk[:, th, :]), "sST", 16)
            blk.sync(p2_st)
            stqk_rel = [("sST", cnt["sST"] - 16), ("sST", cnt["sST"])]
        qk_all_stored = snap("sST")
        if DEBUG_DUMP:
            def p2_dump(sync):
                wt(sync, qk_all_stored)
                inc(sync.dma_start(out=dbg["d_aqk_pre"], in_=aqk_i.ap()), "sST", 16)
            blk.sync(p2_dump)
            qk_all_stored = snap("sST")

        pv_rel = [None, None]
        stv_rel = [None, None]
        for rp in range(4):
            for tt in range(NT):
                def p2v_mm(tensor, rp=rp, tt=tt):
                    if tt == 0:
                        tensor.wait_ge(sems["sGW"], gw_base + 32 * 32 + (rp + 1) * 32)
                    wt(tensor, pv_rel[tt % 2])
                    for cc in range(CCH):
                        ins = tensor.matmul(pb[2 + tt % 2][:],
                                            xnT[:, cc, tt * 128:(tt + 1) * 128],
                                            wv_b[:, rp % 2, cc, :],
                                            start=(cc == 0), stop=(cc == CCH - 1))
                    inc(ins, "sT")
                blk.tensor(p2v_mm)
                mmv = cnt["sT"]
                assert mmv == T0v + rp * 8 + tt + 1

                def p2v_ev(vector, rp=rp, tt=tt, mmv=mmv):
                    vector.wait_ge(sems["sT"], mmv)
                    wt(vector, stv_rel[tt % 2])
                    inc(vector.tensor_tensor(st_v[:, tt % 2, :], pb[2 + tt % 2][:],
                                             rep0[:, rp * 512:(rp + 1) * 512], ALU.add), "sV")
                blk.vector(p2v_ev)
                evv = cnt["sV"]
                pv_rel[tt % 2] = ("sV", evv)

                def p2v_st(sync, rp=rp, tt=tt, evv=evv):
                    sync.wait_ge(sems["sV"], evv)
                    r0 = 2 * rp
                    inc(sync.dma_start(
                        out=av_i.ap()[r0 * 1024 + tt * 128:r0 * 1024 + (tt + 1) * 128, :],
                        in_=st_v[:, tt % 2, 0:256]), "sST", 16)
                    inc(sync.dma_start(
                        out=av_i.ap()[(r0 + 1) * 1024 + tt * 128:(r0 + 1) * 1024 + (tt + 1) * 128, :],
                        in_=st_v[:, tt % 2, 256:512]), "sST", 16)
                blk.sync(p2v_st)
                stv_rel[tt % 2] = ("sST", cnt["sST"])
        v_all_stored = snap("sST")
        p2_evacs_done = snap("sV")
        es1.close()
        if PHASES < 3:
            def fin0(sync):
                sync.wait_ge(sems["sST"], cnt["sST"])
            blk.sync(fin0)
            return nc

        es2 = ExitStack()
        wffn_b = es2.enter_context(nc.sbuf_tensor("wffn_b", [128, CCH, 2048], BF16))
        st_f1 = es2.enter_context(nc.sbuf_tensor("st_f1", [128, 2, 512], F32))
        st_f2 = es2.enter_context(nc.sbuf_tensor("st_f2", [128, 2, 512], F16))

        def p3_w(gpsimd):
            wt(gpsimd, v_all_stored)
            for cc in range(CCH):
                inc(gpsimd.dma_start(out=wffn_b[:, cc, :], in_=wffn[cc]), "sGW", 16)
            for cc in range(CCH):
                inc(gpsimd.dma_start(out=settle_s[:, 0:16],
                                     in_=wffn_b[:, cc, 2032:2048]), "sGW", 16)
        blk.gpsimd(p3_w)
        wffn_ok = snap("sGW")

        def p3_rep(sync):
            wt(sync, v_all_stored)
            inc(sync.dma_start(out=rep0[:], in_=wsc_rep), "sLD", 16)
            inc(sync.dma_start(out=rep1[:], in_=fb_rep), "sLD", 16)
        blk.sync(p3_rep)
        reps_ok = snap("sLD")

        pf_rel = [p2_evacs_done] * 4
        stf_rel = [None, None]
        for tt in range(NT):
            for nb in range(4):
                g = (tt * 4 + nb) % 4

                def p3_mm(tensor, tt=tt, nb=nb, g=g):
                    if tt == 0 and nb == 0:
                        wt(tensor, wffn_ok)
                    wt(tensor, pf_rel[g])
                    for cc in range(CCH):
                        ins = tensor.matmul(pb[g][:], qT_ffn[:, cc, tt * 128:(tt + 1) * 128],
                                            wffn_b[:, cc, nb * 512:(nb + 1) * 512],
                                            start=(cc == 0), stop=(cc == CCH - 1))
                    inc(ins, "sT")
                blk.tensor(p3_mm)
                mmf = cnt["sT"]

                def p3_ev(vector, tt=tt, nb=nb, g=g, mmf=mmf):
                    if tt == 0 and nb == 0:
                        wt(vector, reps_ok)
                    vector.wait_ge(sems["sT"], mmf)
                    wt(vector, stf_rel[nb % 2])
                    vector.tensor_scalar_mul(ev_tmp[:], pb[g][:], insc[:, tt:tt + 1])
                    vector.tensor_tensor(st_f1[:, nb % 2, :], ev_tmp[:],
                                         rep0[:, nb * 512:(nb + 1) * 512], ALU.mult)
                    inc(vector.tensor_tensor(st_f2[:, nb % 2, :], st_f1[:, nb % 2, :],
                                             rep1[:, nb * 512:(nb + 1) * 512], ALU.add), "sV")
                blk.vector(p3_ev)
                evf = cnt["sV"]
                pf_rel[g] = ("sV", evf)

                def p3_st(sync, tt=tt, nb=nb, evf=evf):
                    sync.wait_ge(sems["sV"], evf)
                    inc(sync.dma_start(
                        out=ffn16d.ap()[tt * 128:(tt + 1) * 128, nb * 512:(nb + 1) * 512],
                        in_=st_f2[:, nb % 2, :]), "sST", 16)
                blk.sync(p3_st)
                stf_rel[nb % 2] = ("sST", cnt["sST"])
        ffn_mms_done = snap("sT")
        if DEBUG_DUMP:
            def p3_dump(sync):
                wt(sync, ffn_all_stored if False else snap("sST"))
                inc(sync.dma_start(out=dbg["d_ffn16b"], in_=ffn16d.ap()), "sST", 16)
            blk.sync(p3_dump)
        es2.close()
        if PHASES < 5:
            def fin0(sync):
                sync.wait_ge(sems["sST"], cnt["sST"])
                sync.wait_ge(sems["sCC"], cnt["sCC"])
            blk.sync(fin0)
            return nc
        ffn_all_stored = snap("sST")
        p3_evacs_done = snap("sV")

        def p_coll(gpsimd):
            wt(gpsimd, qk_all_stored)
            wt(gpsimd, ffn_all_stored)
            wt(gpsimd, ("sT", ffn_mms_done[1]))
            with allow_ici_all_to_all():
                inc(gpsimd.collective_compute(
                    "AllToAll", ALU.bypass, replica_groups=[list(range(W))],
                    ins=[aqk_i.ap().opt()], outs=[aqk_o.ap().opt()]), "sCC")
            wt(gpsimd, v_all_stored)
            with allow_ici_all_to_all():
                inc(gpsimd.collective_compute(
                    "AllToAll", ALU.bypass, replica_groups=[list(range(W))],
                    ins=[av_i.ap().opt()], outs=[av_o.ap().opt()]), "sCC")
            inc(gpsimd.collective_compute(
                "AllGather", ALU.bypass, replica_groups=[list(range(W))],
                ins=[bar1_i.ap().opt()], outs=[bar1_o.ap().opt()]), "sCC")
        blk.gpsimd(p_coll)
        if PHASES < 4:
            def fin0(sync):
                sync.wait_ge(sems["sCC"], cnt["sCC"])
            blk.sync(fin0)
            return nc
        cc_qk = ("sCC", 1)
        cc_v = ("sCC", 3)

        es3 = ExitStack()
        qk_buf = es3.enter_context(nc.sbuf_tensor("qk_buf", [128, 2, 2, 2048], BF16))
        v_buf = es3.enter_context(nc.sbuf_tensor("v_buf", [128, 2, CCH, 128], BF16))
        sqs = es3.enter_context(nc.sbuf_tensor("sqs", [128, 512], BF16))
        pT = es3.enter_context(nc.sbuf_tensor("pT", [128, 2, 512], BF16))
        st_av = es3.enter_context(nc.sbuf_tensor("st_av", [128, 2, 512], BF16))
        st_rc = es3.enter_context(nc.sbuf_tensor("st_rc", [128, 2, 512], F32))

        qkv_rel = [None, None]
        stav_rel = [None, None]
        pS_rel = [pf_rel[0], pf_rel[1]]
        pnorm_rel = pf_rel[2]
        prev_av_all = None
        prev_out_ok = None

        for bh in range(8):
            s2 = bh % 2
            b_, hl_ = bh >> 1, bh & 1

            def p4_load(sync, bh=bh, s2=s2, b_=b_, hl_=hl_):
                if bh == 0:
                    wt(sync, cc_qk)
                    wt(sync, cc_v)
                    wt(sync, ffn_mms_done)
                    wt(sync, ffn_all_stored)
                wt(sync, qkv_rel[s2])
                for sblk in range(2):
                    base = (2 * b_ + sblk) * 512 + hl_ * 128
                    inc(sync.dma_start(out=qk_buf[:, s2, 0, sblk * 1024:(sblk + 1) * 1024],
                                       in_=aqk_o.ap()[base:base + 128, :]), "sLD", 16)
                    inc(sync.dma_start(out=qk_buf[:, s2, 1, sblk * 1024:(sblk + 1) * 1024],
                                       in_=aqk_o.ap()[base + 256:base + 384, :]), "sLD", 16)
                for part in range(4):
                    inc(sync.dma_start(
                        out=v_buf[:, s2, part * 4:(part + 1) * 4, :],
                        in_=av_o.ap()[b_ * 2048 + part * 512:b_ * 2048 + (part + 1) * 512,
                                      hl_ * 128:(hl_ + 1) * 128].rearrange(
                            "(j p) d -> p j d", p=128)), "sLD", 16)
            if bh == 0 and DEBUG_DUMP:
                def p4_dump0(sync):
                    wt(sync, cc_qk)
                    wt(sync, cc_v)
                    inc(sync.dma_start(out=dbg["d_aqk_ob"], in_=aqk_o.ap()), "sST", 16)
                blk.sync(p4_dump0)
            blk.sync(p4_load)
            bh_loaded = snap("sLD")

            first_nrm = True
            for qk in range(2):
                for ch in range(4):
                    def p4_sq(vector, s2=s2, qk=qk, ch=ch, first=first_nrm,
                              bh_loaded=bh_loaded):
                        if first:
                            wt(vector, bh_loaded)
                        wt(vector, pnorm_rel)
                        inc(vector.tensor_tensor(
                            sqs[:], qk_buf[:, s2, qk, ch * 512:(ch + 1) * 512],
                            qk_buf[:, s2, qk, ch * 512:(ch + 1) * 512], ALU.mult), "sV")
                    blk.vector(p4_sq)
                    sq_ok = snap("sV")
                    first_nrm = False

                    def p4_nmm(tensor, sq_ok=sq_ok):
                        wt(tensor, sq_ok)
                        wt(tensor, pnorm_rel)
                        inc(tensor.matmul(pb[2][:], onesb_s[:], sqs[:],
                                          start=True, stop=True), "sT")
                    blk.tensor(p4_nmm)
                    nmm_ok = cnt["sT"]

                    def p4_nred(vector, qk=qk, ch=ch, nmm_ok=nmm_ok):
                        vector.wait_ge(sems["sT"], nmm_ok)
                        inc(vector.reduce_max(nrm[:, qk * 4 + ch:qk * 4 + ch + 1],
                                              pb[2][:], axis=AX.X), "sV")
                    blk.vector(p4_nred)
                    pnorm_rel = ("sV", cnt["sV"])

            def p4_bound_v(vector, pnorm_rel=pnorm_rel):
                wt(vector, pnorm_rel)
                sw(vector, vector.reduce_max(nrm[:, 8:9], nrm[:, 0:4], axis=AX.X), "sV")
                sw(vector, vector.reduce_max(nrm[:, 9:10], nrm[:, 4:8], axis=AX.X), "sV")
                inc(vector.tensor_mul(nrm[:, 10:11], nrm[:, 8:9], nrm[:, 9:10]), "sV")
            blk.vector(p4_bound_v)
            bm_ok = snap("sV")

            def p4_bound_s(scalar, bm_ok=bm_ok):
                wt(scalar, bm_ok)
                inc(scalar.activation(nrm[:, 11:12], nrm[:, 10:11], AF.Sqrt,
                                      scale=1.0 / D), "sS")
            blk.scalar(p4_bound_s)
            bs_ok = snap("sS")

            def p4_bound_v2(vector, bs_ok=bs_ok):
                wt(vector, bs_ok)
                inc(vector.tensor_scalar_mul(nrm[:, 11:12], nrm[:, 11:12], -1.01), "sV")
            blk.vector(p4_bound_v2)
            bound_ok = snap("sV")

            for qb in range(4):
                jmax = 4 * (qb + 1)
                exp_cnt_at = {}
                s_done = {}

                def p4_s0(tensor, s2=s2, qb=qb, bound_ok=bound_ok):
                    wt(tensor, bound_ok)
                    wt(tensor, pS_rel[0])
                    inc(tensor.matmul(pb[0][:], qk_buf[:, s2, 1, 0:128],
                                      qk_buf[:, s2, 0, qb * 512:(qb + 1) * 512],
                                      start=True, stop=True), "sT")
                blk.tensor(p4_s0)
                s_done[0] = cnt["sT"]

                for j in range(jmax):
                    diag = j >= 4 * qb
                    if diag:
                        def p4_mask(vector, j=j, qb=qb, sd=s_done[j]):
                            vector.wait_ge(sems["sT"], sd)
                            inc(vector.tensor_tensor(pb[j % 2][:], pb[j % 2][:],
                                                     maskb_s[:, j - 4 * qb, :], ALU.add),
                                "sV")
                        blk.vector(p4_mask)
                        mask_ok = snap("sV")
                    else:
                        mask_ok = None

                    if j >= 1:
                        pt_release = s_guard_av[j - 1]
                    else:
                        pt_release = prev_av_all if prev_av_all else None

                    def p4_exp(scalar, j=j, sd=s_done[j], mask_ok=mask_ok,
                               pt_release=pt_release):
                        scalar.wait_ge(sems["sT"], sd)
                        if mask_ok:
                            wt(scalar, mask_ok)
                        if pt_release is not None:
                            scalar.wait_ge(sems["sT"], pt_release)
                        inc(scalar.activation(pT[:, j % 2, :], pb[j % 2][:], AF.Exp,
                                              bias=nrm[:, 11:12], scale=SCALE_S), "sS")
                    blk.scalar(p4_exp)
                    exp_cnt_at[j] = cnt["sS"]

                    def p4_pe_j(tensor, j=j, s2=s2, qb=qb, jmax=jmax,
                                exp_j=exp_cnt_at[j],
                                exp_jm1=exp_cnt_at.get(j - 1),
                                prev_out_ok=prev_out_ok,
                                pS1_prev=(pS_rel[1] if j == 0 else None)):
                        if j + 1 < jmax:
                            if exp_jm1:
                                tensor.wait_ge(sems["sS"], exp_jm1)
                            if pS1_prev is not None:
                                wt(tensor, pS1_prev)
                            inc(tensor.matmul(pb[(j + 1) % 2][:],
                                              qk_buf[:, s2, 1, (j + 1) * 128:(j + 2) * 128],
                                              qk_buf[:, s2, 0, qb * 512:(qb + 1) * 512],
                                              start=True, stop=True), "sT")
                        tensor.wait_ge(sems["sS"], exp_j)
                        if j == 0 and prev_out_ok is not None:
                            tensor.wait_ge(sems["sV"], prev_out_ok)
                        tensor.matmul(pb[4][:], v_buf[:, s2, j, :], pT[:, j % 2, :],
                                      start=(j == 0), stop=(j == jmax - 1),
                                      skip_group_check=True)
                        inc(tensor.matmul(pb[5][:], onesb_s[:], pT[:, j % 2, :],
                                          start=(j == 0), stop=(j == jmax - 1),
                                          skip_group_check=True), "sT")
                    if j == 0:
                        s_guard_av = {}
                    blk.tensor(p4_pe_j)
                    if j + 1 < jmax:
                        s_done[j + 1] = cnt["sT"] - 1
                    s_guard_av[j] = cnt["sT"]

                prev_av_all = cnt["sT"]
                pS_rel = [("sS", exp_cnt_at[jmax - 2]), ("sS", exp_cnt_at[jmax - 1])]

                def p4_out(vector, qb=qb, av_all=prev_av_all):
                    vector.wait_ge(sems["sT"], av_all)
                    wt(vector, stav_rel[qb % 2])
                    sw(vector, vector.reciprocal(st_rc[:, qb % 2, :], pb[5][:]), "sV")
                    inc(vector.tensor_tensor(st_av[:, qb % 2, :], pb[4][:],
                                             st_rc[:, qb % 2, :], ALU.mult), "sV")
                blk.vector(p4_out)
                prev_out_ok = cnt["sV"]
                out_ok = snap("sV")

                def p4_st(sync, b_=b_, hl_=hl_, qb=qb, out_ok=out_ok):
                    wt(sync, out_ok)
                    r = 2 * b_ + qb // 2
                    inc(sync.dma_start(
                        out=aav_i.ap()[r * 256 + hl_ * 128:r * 256 + (hl_ + 1) * 128,
                                       (qb % 2) * 512:(qb % 2) * 512 + 512],
                        in_=st_av[:, qb % 2, :]), "sST", 16)
                blk.sync(p4_st)
                stav_rel[qb % 2] = ("sST", cnt["sST"])
            qkv_rel[s2] = ("sT", cnt["sT"])
        if PHASES < 6:
            def fin0(sync):
                sync.wait_ge(sems["sST"], cnt["sST"])
                sync.wait_ge(sems["sCC"], cnt["sCC"])
            blk.sync(fin0)
            es3.close()
            return nc
        att_all_stored = snap("sST")
        att_done_s = snap("sS")
        att_done_v = snap("sV")
        es3.close()

        if DEBUG_DUMP:
            def p5_predump(sync):
                wt(sync, att_all_stored)
                inc(sync.dma_start(out=dbg["d_aav_pre"], in_=aav_i.ap()), "sST", 16)
            blk.sync(p5_predump)
            att_all_stored = snap("sST")

        def p5_coll(gpsimd):
            wt(gpsimd, att_all_stored)
            with allow_ici_all_to_all():
                inc(gpsimd.collective_compute(
                    "AllToAll", ALU.bypass, replica_groups=[list(range(W))],
                    ins=[aav_i.ap().opt()], outs=[aav_o.ap().opt()]), "sCC")
            inc(gpsimd.collective_compute(
                "AllGather", ALU.bypass, replica_groups=[list(range(W))],
                ins=[bar2_i.ap().opt()], outs=[bar2_o.ap().opt()]), "sCC")
        blk.gpsimd(p5_coll)
        cc_av = ("sCC", 5)

        es4 = ExitStack()
        wout_b = es4.enter_context(nc.sbuf_tensor("wout_b", [128, CCH, 2048], BF16))
        xep = es4.enter_context(nc.sbuf_tensor("xep", [128, 2, 2048], F32))
        fep = es4.enter_context(nc.sbuf_tensor("fep", [128, 2, 2048], F16))
        oep = es4.enter_context(nc.sbuf_tensor("oep", [128, 2, 2048], F32))

        def p5_w(gpsimd):
            wt(gpsimd, cc_av)
            for cc in range(CCH):
                inc(gpsimd.dma_start(out=wout_b[:, cc, :], in_=wout[cc]), "sGW", 16)
            for cc in range(CCH):
                inc(gpsimd.dma_start(out=settle_s[:, 0:16],
                                     in_=wout_b[:, cc, 2032:2048]), "sGW", 16)
        blk.gpsimd(p5_w)
        wout_ok = snap("sGW")

        def p5_rep(sync):
            wt(sync, ffn_all_stored)
            inc(sync.dma_start(out=rep1[:], in_=outb_rep), "sLD", 16)
        blk.sync(p5_rep)
        outb_ok = snap("sLD")

        def p5_at(sync):
            wt(sync, cc_av)
            wt(sync, ffn_all_stored)
            for g in range(CCH):
                inc(sync.dma_start(out=qT_ffn[:, g, :],
                                   in_=aav_o.ap()[g * 128:(g + 1) * 128, :]), "sLD", 16)
        blk.sync(p5_at)
        at_ok = snap("sLD")

        po_rel = [None] * 4
        ep_rel = [None, None]
        for tt in range(NT):
            def p5_ld(sync, tt=tt):
                wt(sync, ep_rel[tt % 2])
                inc(sync.dma_start(out=xep[:, tt % 2, :],
                                   in_=x_own[tt * 128:(tt + 1) * 128, :]), "sLD", 16)
                inc(sync.dma_start(out=fep[:, tt % 2, :],
                                   in_=ffn16d.ap()[tt * 128:(tt + 1) * 128, :]), "sLD", 16)
            blk.sync(p5_ld)
            ep_ld = snap("sLD")
            oep_free = ep_rel[tt % 2]

            for fb in range(4):
                g4 = fb % 4

                def p5_mm(tensor, tt=tt, fb=fb, g4=g4):
                    if tt == 0 and fb == 0:
                        wt(tensor, wout_ok)
                        wt(tensor, at_ok)
                        wt(tensor, att_done_s)
                        wt(tensor, att_done_v)
                    wt(tensor, po_rel[g4])
                    for g in range(CCH):
                        ins = tensor.matmul(pb[g4][:], qT_ffn[:, g, tt * 128:(tt + 1) * 128],
                                            wout_b[:, g, fb * 512:(fb + 1) * 512],
                                            start=(g == 0), stop=(g == CCH - 1))
                    inc(ins, "sT")
                blk.tensor(p5_mm)
                mmo = cnt["sT"]

                def p5_ep(vector, tt=tt, fb=fb, g4=g4, mmo=mmo, ep_ld=ep_ld,
                          oep_free=oep_free):
                    if tt == 0 and fb == 0:
                        wt(vector, outb_ok)
                    if fb == 0:
                        wt(vector, oep_free)
                    vector.wait_ge(sems["sT"], mmo)
                    wt(vector, ep_ld)
                    sl = slice(fb * 512, (fb + 1) * 512)
                    vector.tensor_tensor(oep[:, tt % 2, sl], pb[g4][:], rep1[:, sl], ALU.add)
                    vector.tensor_tensor(oep[:, tt % 2, sl], oep[:, tt % 2, sl],
                                         xep[:, tt % 2, sl], ALU.add)
                    inc(vector.tensor_tensor(oep[:, tt % 2, sl], oep[:, tt % 2, sl],
                                             fep[:, tt % 2, sl], ALU.add), "sV")
                blk.vector(p5_ep)
                po_rel[g4] = ("sV", cnt["sV"])
            ep_done = snap("sV")

            def p5_out(sync, tt=tt, ep_done=ep_done):
                wt(sync, ep_done)
                inc(sync.dma_start(out=y[tt * 128:(tt + 1) * 128, :],
                                   in_=oep[:, tt % 2, :]), "sST", 16)
            blk.sync(p5_out)
            ep_rel[tt % 2] = ("sST", cnt["sST"])

        def fin(sync):
            sync.wait_ge(sems["sST"], cnt["sST"])
            sync.wait_ge(sems["sCC"], cnt["sCC"])
            if DEBUG_DUMP:
                for nm, srct in [("d_aqk_i", aqk_i), ("d_aqk_o", aqk_o), ("d_av_i", av_i),
                                 ("d_av_o", av_o), ("d_aav_i", aav_i), ("d_aav_o", aav_o),
                                 ("d_ffn16", ffn16d)]:
                    inc(sync.dma_start(out=dbg[nm], in_=srct.ap()), "sST", 16)
                sync.wait_ge(sems["sST"], cnt["sST"])
        blk.sync(fin)
        es4.close()

    return nc


_NC_CACHE = {}


def _host_prep(L0_attn_x, L0_attn_qkv_w, L0_attn_qkv_b, L0_attn_out_w,
               L0_attn_out_b, L0_ffn_weight, L0_ffn_w_scale, L0_ffn_bias):
    X = np.ascontiguousarray(np.asarray(L0_attn_x, np.float32).reshape(TOK, C))
    qkv_w = np.asarray(L0_attn_qkv_w, np.float32)
    qkv_b = np.asarray(L0_attn_qkv_b, np.float32)
    out_w = np.asarray(L0_attn_out_w, np.float32)
    out_b = np.asarray(L0_attn_out_b, np.float32)
    fw = np.asarray(L0_ffn_weight, np.int8)
    wsc = np.asarray(L0_ffn_w_scale, np.float32)[:, 0]
    fb = np.asarray(L0_ffn_bias, np.float32)

    rows, brows = [], []
    for r in range(W):
        for base in (0, 2048):
            for hh in (2 * r, 2 * r + 1):
                rows.append(qkv_w[base + hh * 128: base + (hh + 1) * 128])
                brows.append(qkv_b[base + hh * 128: base + (hh + 1) * 128])
    Wsel = np.concatenate(rows, axis=0)
    bsel = np.concatenate(brows, axis=0)
    wqk = np.ascontiguousarray(Wsel.T.reshape(CCH, 128, 32, 128).transpose(2, 1, 0, 3))
    bqk = np.ascontiguousarray(bsel.reshape(32, 128).T)

    Wv = qkv_w[4096:6144]
    wv = np.ascontiguousarray(Wv.T.reshape(CCH, 128, 4, 512).transpose(2, 1, 0, 3))
    vb_rep = np.ascontiguousarray(np.broadcast_to(qkv_b[4096:6144], (128, 2048)))

    wffn = np.ascontiguousarray(fw.T.reshape(CCH, 128, 2048))
    wsc_rep = np.ascontiguousarray(np.broadcast_to(wsc, (128, 2048)))
    fb_rep = np.ascontiguousarray(np.broadcast_to(fb, (128, 2048)))
    wout = np.ascontiguousarray(out_w.T.reshape(CCH, 128, 2048))
    outb_rep = np.ascontiguousarray(np.broadcast_to(out_b, (128, 2048)))

    tpos = np.arange(T, dtype=np.float64)
    freqs = np.arange(0, D, 2, dtype=np.float64) / D
    ang = tpos[:, None] * freqs[None, :]
    emb = np.concatenate([np.sin(ang), np.cos(ang)], axis=1).astype(np.float32)
    embT = np.ascontiguousarray(emb.T)

    jj = np.arange(128)[:, None]
    qq = np.arange(512)[None, :]
    mask = np.zeros((128, 4, 512), np.float32)
    for v in range(4):
        mask[:, v, :] = np.where(qq >= jj + v * 128, 0.0, NEGBIG)
    identf = np.eye(128, dtype=np.float32)
    identb = np.eye(128, dtype=np.float32).astype(ml_dtypes.bfloat16)
    onesb = np.ones((128, 128), np.float32).astype(ml_dtypes.bfloat16)

    kconst = np.zeros((128, 2), np.float32)
    kconst[:, 0] = 1e-5
    shared = dict(wqk=wqk, bqk=bqk, wv=wv, vb_rep=vb_rep, wffn=wffn, kconst=kconst,
                  wsc_rep=wsc_rep, fb_rep=fb_rep, wout=wout, outb_rep=outb_rep,
                  maskb=mask, identf=identf, identb=identb, onesb=onesb)
    in_maps = []
    for c in range(W):
        m = dict(shared)
        m["x_own"] = np.ascontiguousarray(X[c * TOWN:(c + 1) * TOWN])
        pos0 = (c % 2) * TOWN
        m["embt"] = np.ascontiguousarray(embT[:, pos0:pos0 + TOWN])
        in_maps.append(m)
    return in_maps


def kernel(L0_attn_x, L0_attn_qkv_w, L0_attn_qkv_b, L0_attn_out_w, L0_attn_out_b,
           L0_ffn_weight, L0_ffn_w_scale, L0_ffn_bias):
    in_maps = _host_prep(L0_attn_x, L0_attn_qkv_w, L0_attn_qkv_b, L0_attn_out_w,
                         L0_attn_out_b, L0_ffn_weight, L0_ffn_w_scale, L0_ffn_bias)
    if "nc" not in _NC_CACHE:
        _NC_CACHE["nc"] = build()
    nc = _NC_CACHE["nc"]
    res = run_bass_kernel_spmd(nc, in_maps, core_ids=list(range(W)))
    out = np.concatenate([np.asarray(r["y"]) for r in res.results], axis=0)
    return out.reshape(B, T, C)


# version 21
# speedup vs baseline: 1.1430x; 1.1430x over previous
import numpy as np
import ml_dtypes
from contextlib import ExitStack

import concourse.bass as bass
import concourse.mybir as mybir
from concourse.bass_utils import run_bass_kernel_spmd
from concourse.aot_env import allow_ici_all_to_all

F32 = mybir.dt.float32
BF16 = mybir.dt.bfloat16
F16 = mybir.dt.float16
I8 = mybir.dt.int8
I32 = mybir.dt.int32
AF = mybir.ActivationFunctionType
ALU = mybir.AluOpType
AX = mybir.AxisListType

B, T, C, D = 4, 2048, 2048, 128
W = 8
TOK = B * T
TOWN = TOK // W
NT = TOWN // 128
CCH = C // 128
SCALE_S = float(1.0 / np.sqrt(D))
THETA = 0.4999999701976776
NEGBIG = -1.0e30
PHASES = 99
P1_QUANT = True
P1_TP = True
DEBUG_DUMP = False


def build():
    nc = bass.Bass("TRN2", target_bir_lowering=False, detect_race_conditions=False)

    x_own = nc.dram_tensor("x_own", [TOWN, C], F32, kind="ExternalInput").ap()
    wqk = nc.dram_tensor("wqk", [32, 128, CCH, 128], F32, kind="ExternalInput").ap()
    bqk = nc.dram_tensor("bqk", [128, 32], F32, kind="ExternalInput").ap()
    wv = nc.dram_tensor("wv", [4, 128, CCH, 512], F32, kind="ExternalInput").ap()
    vb_rep = nc.dram_tensor("vb_rep", [128, 2048], F32, kind="ExternalInput").ap()
    wffn = nc.dram_tensor("wffn", [CCH, 128, 2048], I8, kind="ExternalInput").ap()
    wsc_rep = nc.dram_tensor("wsc_rep", [128, 2048], F32, kind="ExternalInput").ap()
    fb_rep = nc.dram_tensor("fb_rep", [128, 2048], F32, kind="ExternalInput").ap()
    wout = nc.dram_tensor("wout", [CCH, 128, 2048], F32, kind="ExternalInput").ap()
    outb_rep = nc.dram_tensor("outb_rep", [128, 2048], F32, kind="ExternalInput").ap()
    embt = nc.dram_tensor("embt", [128, TOWN], F32, kind="ExternalInput").ap()
    maskb = nc.dram_tensor("maskb", [128, 4, 512], F32, kind="ExternalInput").ap()
    identf = nc.dram_tensor("identf", [128, 128], F32, kind="ExternalInput").ap()
    identb = nc.dram_tensor("identb", [128, 128], BF16, kind="ExternalInput").ap()
    onesb = nc.dram_tensor("onesb", [128, 128], BF16, kind="ExternalInput").ap()
    kconst = nc.dram_tensor("kconst", [128, 2], F32, kind="ExternalInput").ap()

    y = nc.dram_tensor("y", [TOWN, C], F32, kind="ExternalOutput").ap()

    ffn16d = nc.dram_tensor("ffn16d", [TOWN, C], F16)
    PAD = [4 * 1024 * 1024]
    nc.dram_tensor("pad_a", PAD, F32)
    aqk_i = nc.dram_tensor("aqk_i", [4096, TOWN], BF16)
    nc.dram_tensor("pad_b", PAD, F32)
    aqk_o = nc.dram_tensor("aqk_o", [4096, TOWN], BF16)
    nc.dram_tensor("pad_c", PAD, F32)
    av_i = nc.dram_tensor("av_i", [TOK, 256], BF16)
    nc.dram_tensor("pad_d", PAD, F32)
    av_o = nc.dram_tensor("av_o", [TOK, 256], BF16)
    nc.dram_tensor("pad_e", PAD, F32)
    aav_i = nc.dram_tensor("aav_i", [2048, TOWN], BF16)
    nc.dram_tensor("pad_f", PAD, F32)
    aav_o = nc.dram_tensor("aav_o", [2048, TOWN], BF16)
    nc.dram_tensor("pad_g", PAD, F32)
    bar1_i = nc.dram_tensor("bar1_i", [16, 64], F32)
    nc.dram_tensor("pad_h", PAD, F32)
    bar1_o = nc.dram_tensor("bar1_o", [128, 64], F32)
    nc.dram_tensor("pad_i", PAD, F32)
    bar2_i = nc.dram_tensor("bar2_i", [16, 64], F32)
    nc.dram_tensor("pad_j", PAD, F32)
    bar2_o = nc.dram_tensor("bar2_o", [128, 64], F32)
    nc.dram_tensor("pad_k", PAD, F32)
    dbg = {}
    if DEBUG_DUMP:
        for nm, shape, dt in [("d_aqk_i", [4096, TOWN], BF16), ("d_aqk_o", [4096, TOWN], BF16),
                              ("d_av_i", [TOK, 256], BF16), ("d_av_o", [TOK, 256], BF16),
                              ("d_aav_i", [2048, TOWN], BF16), ("d_aav_o", [2048, TOWN], BF16),
                              ("d_ffn16", [TOWN, C], F16),
                              ("d_xnT", [CCH, 128, TOWN], BF16),
                              ("d_qT", [CCH, 128, TOWN], BF16),
                              ("d_qst", [128, 2048], F32),
                              ("d_vq", [128, 2048], F32),
                              ("d_tq", [128, 2048], I32),
                              ("d_dq", [128, 2048], F32),
                              ("d_xn7", [128, 2048], F32),
                              ("d_aqk_pre", [4096, TOWN], BF16),
                              ("d_ffn16b", [TOWN, C], F16),
                              ("d_aqk_ob", [4096, TOWN], BF16)]:
            dbg[nm] = nc.dram_tensor(nm, shape, dt, kind="ExternalOutput").ap()

    es = ExitStack()
    sb = lambda name, shape, dt: es.enter_context(nc.sbuf_tensor(name, shape, dt))

    embt_s = sb("embt_s", [128, TOWN], F32)
    bqk_s = sb("bqk_s", [128, 32], F32)
    maskb_s = sb("maskb_s", [128, 4, 512], F32)
    identf_s = sb("identf_s", [128, 128], F32)
    identb_s = sb("identb_s", [128, 128], BF16)
    onesb_s = sb("onesb_s", [128, 128], BF16)
    insc = sb("insc", [128, NT], F32)
    rep0 = sb("rep0", [128, 2048], F32)
    rep1 = sb("rep1", [128, 2048], F32)
    qT_ffn = sb("qT_ffn", [128, CCH, TOWN], BF16)
    st_qk = sb("st_qk", [128, 2, 512], BF16)
    st_v = sb("st_v", [128, 2, 512], BF16)
    nrm = sb("nrm", [128, 12], F32)
    ln1 = sb("ln1", [128, 16], F32)
    kconst_s = sb("kconst_s", [128, 2], F32)
    ev_tmp = sb("ev_tmp", [128, 512], F32)
    settle_s = sb("settle_s", [128, 32], BF16)

    pb = [es.enter_context(nc.psum_tensor(f"pb{i}", [128, 512], F32)) for i in range(8)]

    cnt = {}
    sems = {}
    for name in ["sLD", "sGW", "sST", "sV", "sS", "sT", "sCC"]:
        sems[name] = es.enter_context(nc.semaphore(name))
        cnt[name] = 0

    def inc(ins, sem, k=1):
        cnt[sem] += k
        return ins.then_inc(sems[sem], k)

    def snap(sem):
        return (sem, cnt[sem])

    def wt(eng, rel):
        if rel is not None and rel[1] > 0:
            eng.wait_ge(sems[rel[0]], rel[1])

    def sw(eng, ins, sem):
        inc(ins, sem)
        eng.wait_ge(sems[sem], cnt[sem])

    with es, nc.Block() as blk:
        def p0(sync):
            inc(sync.dma_start(out=embt_s[:], in_=embt), "sLD", 16)
            inc(sync.dma_start(out=bqk_s[:], in_=bqk), "sLD", 16)
            inc(sync.dma_start(out=maskb_s[:], in_=maskb), "sLD", 16)
            inc(sync.dma_start(out=identf_s[:], in_=identf), "sLD", 16)
            inc(sync.dma_start(out=identb_s[:], in_=identb), "sLD", 16)
            inc(sync.dma_start(out=onesb_s[:], in_=onesb), "sLD", 16)
            inc(sync.dma_start(out=rep0[:], in_=vb_rep), "sLD", 16)
            inc(sync.dma_start(out=kconst_s[:], in_=kconst), "sLD", 16)
        blk.sync(p0)
        consts_ok = snap("sLD")

        es1 = ExitStack()
        sb1 = lambda n, s, d: es1.enter_context(nc.sbuf_tensor(n, s, d))
        xst = sb1("xst", [128, 2, 2048], F32)
        xnst = sb1("xnst", [128, 2, 2048], F32)
        qst = sb1("qst", [128, 2048], F32)
        xnT = sb1("xnT", [128, CCH, TOWN], BF16)
        wqk_b = sb1("wqk_b", [128, 2, CCH, 128], BF16)
        wv_b = sb1("wv_b", [128, 2, CCH, 512], BF16)
        es1a = ExitStack()
        sb1a = lambda n, s, d: es1a.enter_context(nc.sbuf_tensor(n, s, d))
        vq_s = sb1a("vq_s", [128, 2048], F32)
        tq_s = sb1a("tq_s", [128, 2048], I32)
        dq_s = sb1a("dq_s", [128, 2048], F32)

        x_rel = [None, None]
        xn_rel = [None, None]
        tslot_rel = [None, None]

        for i in range(NT):
            s = i % 2
            o = s * 8
            qst_rel = snap("sT")

            def p1_load(sync, i=i, s=s):
                wt(sync, x_rel[s])
                inc(sync.dma_start(out=xst[:, s, :],
                                   in_=x_own[i * 128:(i + 1) * 128, :]), "sLD", 16)
            blk.sync(p1_load)
            x_ready = snap("sLD")

            def p1_dve1(vector, s=s, o=o, x_ready=x_ready):
                wt(vector, x_ready)
                sw(vector, vector.reduce_sum(ln1[:, o + 0:o + 1], xst[:, s, :],
                                             axis=AX.X), "sV")
                inc(vector.tensor_scalar_mul(ln1[:, o + 1:o + 2], ln1[:, o + 0:o + 1],
                                             -1.0 / 2048.0), "sV")
            blk.vector(p1_dve1)
            negm_ok = snap("sV")

            def p1_sc1(scalar, s=s, o=o, negm_ok=negm_ok):
                wt(scalar, negm_ok)
                sw(scalar, scalar.activation(dq_s[:], xst[:, s, :], AF.Square,
                                  bias=ln1[:, o + 1:o + 2],
                                  accum_out=ln1[:, o + 2:o + 3]), "sS")
                inc(scalar.activation(ln1[:, o + 3:o + 4], ln1[:, o + 2:o + 3], AF.Sqrt,
                                      bias=kconst_s[:, 0:1], scale=1.0 / 2048.0), "sS")
            blk.scalar(p1_sc1)
            sd_ok = snap("sS")

            def p1_dve2(vector, i=i, s=s, o=o, sd_ok=sd_ok, qst_rel=qst_rel):
                wt(vector, sd_ok)
                sw(vector, vector.reciprocal(ln1[:, o + 4:o + 5], ln1[:, o + 3:o + 4]),
                   "sV")
                wt(vector, xn_rel[s])
                inc(vector.tensor_scalar(xnst[:, s, :], xst[:, s, :], ln1[:, o + 1:o + 2],
                                         ln1[:, o + 4:o + 5], ALU.add, ALU.mult), "sV")
                sw(vector, vector.reduce_max(ln1[:, o + 5:o + 6], xnst[:, s, :],
                                             axis=AX.X, apply_absolute_value=True), "sV")
                sw(vector, vector.tensor_scalar_max(ln1[:, o + 6:o + 7],
                                                    ln1[:, o + 5:o + 6], 1e-7), "sV")
                vector.tensor_scalar_mul(insc[:, i:i + 1], ln1[:, o + 6:o + 7], 1.0 / 127.0)
                if not P1_QUANT:
                    inc(vector.tensor_copy(qst[:], xnst[:, s, :]), "sV")
                    return
                sw(vector, vector.reciprocal(ln1[:, o + 7:o + 8], ln1[:, o + 6:o + 7]),
                   "sV")
                sw(vector, vector.tensor_scalar_mul(ln1[:, o + 7:o + 8],
                                                    ln1[:, o + 7:o + 8], -127.0), "sV")
                vector.tensor_scalar_mul(vq_s[:], xnst[:, s, :], ln1[:, o + 7:o + 8])
                vector.tensor_copy(tq_s[:], vq_s[:])
                vector.tensor_sub(dq_s[:], vq_s[:], tq_s[:])
                vector.scalar_tensor_tensor(vq_s[:], dq_s[:], THETA, tq_s[:],
                                            ALU.is_gt, ALU.add)
                wt(vector, qst_rel)
                inc(vector.scalar_tensor_tensor(qst[:], dq_s[:], -THETA, vq_s[:],
                                                ALU.is_lt, ALU.subtract), "sV")
            blk.vector(p1_dve2)
            xn_ok = snap("sV")

            vbase = cnt["sV"]
            tslot_prev = list(tslot_rel)
            t0 = cnt["sT"]
            if not P1_TP:
                x_rel[s] = xn_ok
                xn_rel[s] = xn_ok
                continue

            def p1_pe(tensor, s=s, vbase=vbase, tslot_prev=tslot_prev, xn_ok=xn_ok):
                wt(tensor, consts_ok)
                wt(tensor, xn_ok)
                for g in range(8):
                    src = xnst[:, s, :] if g < 4 else qst[:]
                    jb = (g % 4) * 4
                    bank = pb[6 + g % 2]
                    if g < 2:
                        wt(tensor, tslot_prev[g % 2])
                    else:
                        tensor.wait_ge(sems["sV"], vbase + g - 1)
                    for k in range(4):
                        ins = tensor.transpose(bank[:, k * 128:(k + 1) * 128],
                                               src[:, (jb + k) * 128:(jb + k + 1) * 128],
                                               identf_s[:])
                    inc(ins, "sT")
            blk.tensor(p1_pe)

            def p1_evac(vector, i=i, t0=t0):
                for g in range(8):
                    jb = (g % 4) * 4
                    bank = pb[6 + g % 2].ap().rearrange("p (a b) -> p a b", a=4)
                    dst = xnT if g < 4 else qT_ffn
                    vector.wait_ge(sems["sT"], t0 + g + 1)
                    inc(vector.tensor_copy(dst[:, jb:jb + 4, i * 128:(i + 1) * 128],
                                           bank), "sV")
            blk.vector(p1_evac)
            tslot_rel = [("sV", vbase + 7), ("sV", vbase + 8)]
            x_rel[s] = xn_ok
            xn_rel[s] = snap("sV")

        p1_done_v = snap("sV")
        if DEBUG_DUMP:
            def p1_dump(sync):
                wt(sync, p1_done_v)
                inc(sync.dma_start(out=dbg["d_qst"], in_=qst[:]), "sST", 16)
                inc(sync.dma_start(out=dbg["d_vq"], in_=vq_s[:]), "sST", 16)
                inc(sync.dma_start(out=dbg["d_tq"], in_=tq_s[:]), "sST", 16)
                inc(sync.dma_start(out=dbg["d_dq"], in_=dq_s[:]), "sST", 16)
                inc(sync.dma_start(out=dbg["d_xn7"], in_=xnst[:, 1, :]), "sST", 16)
                inc(sync.dma_start(out=dbg["d_xnT"].rearrange("a p b -> p a b"),
                                   in_=xnT[:]), "sST", 16)
                inc(sync.dma_start(out=dbg["d_qT"].rearrange("a p b -> p a b"),
                                   in_=qT_ffn[:]), "sST", 16)
            blk.sync(p1_dump)
            def p1_dump_wait(sync):
                sync.wait_ge(sems["sST"], cnt["sST"])
            blk.sync(p1_dump_wait)
        es1a.close()
        if PHASES < 2:
            es1.close()
            def fin0(sync):
                sync.wait_ge(sems["sV"], cnt["sV"])
            blk.sync(fin0)
            return nc

        T0 = cnt["sT"]
        T0v = T0 + 64
        gw_base = cnt["sGW"]

        def p2_w(gpsimd):
            for fc in range(32):
                if fc >= 2:
                    gpsimd.wait_ge(sems["sT"], T0 + 2 * (fc - 1))
                inc(gpsimd.dma_start(out=wqk_b[:, fc % 2, :, :], in_=wqk[fc]), "sGW", 16)
                inc(gpsimd.dma_start(out=settle_s[:, 0:16],
                                     in_=wqk_b[:, fc % 2, CCH - 1, 112:128]), "sGW", 16)
            for rp in range(4):
                if rp >= 2:
                    gpsimd.wait_ge(sems["sT"], T0v + (rp - 1) * 8)
                inc(gpsimd.dma_start(out=wv_b[:, rp % 2, :, :], in_=wv[rp]), "sGW", 16)
                inc(gpsimd.dma_start(out=settle_s[:, 16:32],
                                     in_=wv_b[:, rp % 2, CCH - 1, 496:512]), "sGW", 16)
        blk.gpsimd(p2_w)

        pqk_rel = [None, None]
        stqk_rel = [None, None]
        for fc in range(32):
            def p2_mm(tensor, fc=fc):
                if fc == 0:
                    wt(tensor, p1_done_v)
                tensor.wait_ge(sems["sGW"], gw_base + (fc + 1) * 32)
                for th in range(2):
                    wt(tensor, pqk_rel[th])
                    for cc in range(CCH):
                        ins = tensor.matmul(pb[th][:], wqk_b[:, fc % 2, cc, :],
                                            xnT[:, cc, th * 512:(th + 1) * 512],
                                            start=(cc == 0), stop=(cc == CCH - 1))
                    inc(ins, "sT")
            blk.tensor(p2_mm)
            mm_now = cnt["sT"]
            assert mm_now == T0 + 2 * (fc + 1)

            def p2_ev(vector, fc=fc, mm_now=mm_now):
                if fc == 0:
                    wt(vector, consts_ok)
                for th in range(2):
                    vector.wait_ge(sems["sT"], mm_now - 1 + th)
                    wt(vector, stqk_rel[th])
                    vector.tensor_scalar_add(ev_tmp[:], pb[th][:],
                                             bqk_s[:, fc:fc + 1])
                    inc(vector.tensor_tensor(
                        st_qk[:, th, :], ev_tmp[:],
                        embt_s[:, th * 512:(th + 1) * 512], ALU.mult), "sV")
            blk.vector(p2_ev)
            ev_now = cnt["sV"]
            pqk_rel = [("sV", ev_now - 1), ("sV", ev_now)]

            def p2_st(sync, fc=fc, ev_now=ev_now):
                for th in range(2):
                    sync.wait_ge(sems["sV"], ev_now - 1 + th)
                    inc(sync.dma_start(
                        out=aqk_i.ap()[fc * 128:(fc + 1) * 128, th * 512:(th + 1) * 512],
                        in_=st_qk[:, th, :]), "sST", 16)
            blk.sync(p2_st)
            stqk_rel = [("sST", cnt["sST"] - 16), ("sST", cnt["sST"])]
        qk_all_stored = snap("sST")
        if DEBUG_DUMP:
            def p2_dump(sync):
                wt(sync, qk_all_stored)
                inc(sync.dma_start(out=dbg["d_aqk_pre"], in_=aqk_i.ap()), "sST", 16)
            blk.sync(p2_dump)
            qk_all_stored = snap("sST")

        pv_rel = [None, None]
        stv_rel = [None, None]
        for rp in range(4):
            for tt in range(NT):
                def p2v_mm(tensor, rp=rp, tt=tt):
                    if tt == 0:
                        tensor.wait_ge(sems["sGW"], gw_base + 32 * 32 + (rp + 1) * 32)
                    wt(tensor, pv_rel[tt % 2])
                    for cc in range(CCH):
                        ins = tensor.matmul(pb[2 + tt % 2][:],
                                            xnT[:, cc, tt * 128:(tt + 1) * 128],
                                            wv_b[:, rp % 2, cc, :],
                                            start=(cc == 0), stop=(cc == CCH - 1))
                    inc(ins, "sT")
                blk.tensor(p2v_mm)
                mmv = cnt["sT"]
                assert mmv == T0v + rp * 8 + tt + 1

                def p2v_ev(vector, rp=rp, tt=tt, mmv=mmv):
                    vector.wait_ge(sems["sT"], mmv)
                    wt(vector, stv_rel[tt % 2])
                    inc(vector.tensor_tensor(st_v[:, tt % 2, :], pb[2 + tt % 2][:],
                                             rep0[:, rp * 512:(rp + 1) * 512], ALU.add), "sV")
                blk.vector(p2v_ev)
                evv = cnt["sV"]
                pv_rel[tt % 2] = ("sV", evv)

                def p2v_st(sync, rp=rp, tt=tt, evv=evv):
                    sync.wait_ge(sems["sV"], evv)
                    r0 = 2 * rp
                    inc(sync.dma_start(
                        out=av_i.ap()[r0 * 1024 + tt * 128:r0 * 1024 + (tt + 1) * 128, :],
                        in_=st_v[:, tt % 2, 0:256]), "sST", 16)
                    inc(sync.dma_start(
                        out=av_i.ap()[(r0 + 1) * 1024 + tt * 128:(r0 + 1) * 1024 + (tt + 1) * 128, :],
                        in_=st_v[:, tt % 2, 256:512]), "sST", 16)
                blk.sync(p2v_st)
                stv_rel[tt % 2] = ("sST", cnt["sST"])
        v_all_stored = snap("sST")
        p2_evacs_done = snap("sV")
        es1.close()
        if PHASES < 3:
            def fin0(sync):
                sync.wait_ge(sems["sST"], cnt["sST"])
            blk.sync(fin0)
            return nc

        es2 = ExitStack()
        wffn_b = es2.enter_context(nc.sbuf_tensor("wffn_b", [128, CCH, 2048], BF16))
        st_f1 = es2.enter_context(nc.sbuf_tensor("st_f1", [128, 2, 512], F32))
        st_f2 = es2.enter_context(nc.sbuf_tensor("st_f2", [128, 2, 512], F16))

        def p3_w(gpsimd):
            wt(gpsimd, v_all_stored)
            for cc in range(CCH):
                inc(gpsimd.dma_start(out=wffn_b[:, cc, :], in_=wffn[cc]), "sGW", 16)
            for cc in range(CCH):
                inc(gpsimd.dma_start(out=settle_s[:, 0:16],
                                     in_=wffn_b[:, cc, 2032:2048]), "sGW", 16)
        blk.gpsimd(p3_w)
        wffn_ok = snap("sGW")

        def p3_rep(sync):
            wt(sync, v_all_stored)
            inc(sync.dma_start(out=rep0[:], in_=wsc_rep), "sLD", 16)
            inc(sync.dma_start(out=rep1[:], in_=fb_rep), "sLD", 16)
        blk.sync(p3_rep)
        reps_ok = snap("sLD")

        pf_rel = [p2_evacs_done] * 4
        stf_rel = [None, None]
        for tt in range(NT):
            for nb in range(4):
                g = (tt * 4 + nb) % 4

                def p3_mm(tensor, tt=tt, nb=nb, g=g):
                    if tt == 0 and nb == 0:
                        wt(tensor, wffn_ok)
                    wt(tensor, pf_rel[g])
                    for cc in range(CCH):
                        ins = tensor.matmul(pb[g][:], qT_ffn[:, cc, tt * 128:(tt + 1) * 128],
                                            wffn_b[:, cc, nb * 512:(nb + 1) * 512],
                                            start=(cc == 0), stop=(cc == CCH - 1))
                    inc(ins, "sT")
                blk.tensor(p3_mm)
                mmf = cnt["sT"]

                def p3_ev(vector, tt=tt, nb=nb, g=g, mmf=mmf):
                    if tt == 0 and nb == 0:
                        wt(vector, reps_ok)
                    vector.wait_ge(sems["sT"], mmf)
                    wt(vector, stf_rel[nb % 2])
                    vector.tensor_scalar_mul(ev_tmp[:], pb[g][:], insc[:, tt:tt + 1])
                    vector.tensor_tensor(st_f1[:, nb % 2, :], ev_tmp[:],
                                         rep0[:, nb * 512:(nb + 1) * 512], ALU.mult)
                    inc(vector.tensor_tensor(st_f2[:, nb % 2, :], st_f1[:, nb % 2, :],
                                             rep1[:, nb * 512:(nb + 1) * 512], ALU.add), "sV")
                blk.vector(p3_ev)
                evf = cnt["sV"]
                pf_rel[g] = ("sV", evf)

                def p3_st(sync, tt=tt, nb=nb, evf=evf):
                    sync.wait_ge(sems["sV"], evf)
                    inc(sync.dma_start(
                        out=ffn16d.ap()[tt * 128:(tt + 1) * 128, nb * 512:(nb + 1) * 512],
                        in_=st_f2[:, nb % 2, :]), "sST", 16)
                blk.sync(p3_st)
                stf_rel[nb % 2] = ("sST", cnt["sST"])
        ffn_mms_done = snap("sT")
        if DEBUG_DUMP:
            def p3_dump(sync):
                wt(sync, ffn_all_stored if False else snap("sST"))
                inc(sync.dma_start(out=dbg["d_ffn16b"], in_=ffn16d.ap()), "sST", 16)
            blk.sync(p3_dump)
        es2.close()
        if PHASES < 5:
            def fin0(sync):
                sync.wait_ge(sems["sST"], cnt["sST"])
                sync.wait_ge(sems["sCC"], cnt["sCC"])
            blk.sync(fin0)
            return nc
        ffn_all_stored = snap("sST")
        p3_evacs_done = snap("sV")

        def p_coll(gpsimd):
            wt(gpsimd, qk_all_stored)
            wt(gpsimd, ffn_all_stored)
            wt(gpsimd, ("sT", ffn_mms_done[1]))
            with allow_ici_all_to_all():
                inc(gpsimd.collective_compute(
                    "AllToAll", ALU.bypass, replica_groups=[list(range(W))],
                    ins=[aqk_i.ap().opt()], outs=[aqk_o.ap().opt()]), "sCC")
            wt(gpsimd, v_all_stored)
            with allow_ici_all_to_all():
                inc(gpsimd.collective_compute(
                    "AllToAll", ALU.bypass, replica_groups=[list(range(W))],
                    ins=[av_i.ap().opt()], outs=[av_o.ap().opt()]), "sCC")
            inc(gpsimd.collective_compute(
                "AllGather", ALU.bypass, replica_groups=[list(range(W))],
                ins=[bar1_i.ap().opt()], outs=[bar1_o.ap().opt()]), "sCC")
        blk.gpsimd(p_coll)
        if PHASES < 4:
            def fin0(sync):
                sync.wait_ge(sems["sCC"], cnt["sCC"])
            blk.sync(fin0)
            return nc
        cc_qk = ("sCC", 1)
        cc_v = ("sCC", 3)

        es3 = ExitStack()
        qk_buf = es3.enter_context(nc.sbuf_tensor("qk_buf", [128, 2, 2, 2048], BF16))
        v_buf = es3.enter_context(nc.sbuf_tensor("v_buf", [128, 2, CCH, 128], BF16))
        sqs = es3.enter_context(nc.sbuf_tensor("sqs", [128, 512], BF16))
        pT = es3.enter_context(nc.sbuf_tensor("pT", [128, 2, 512], BF16))
        st_av = es3.enter_context(nc.sbuf_tensor("st_av", [128, 2, 512], BF16))
        st_rc = es3.enter_context(nc.sbuf_tensor("st_rc", [128, 2, 512], F32))

        qkv_rel = [None, None]
        stav_rel = [None, None]
        pS_rel = [pf_rel[0], pf_rel[1]]
        pnorm_rel = pf_rel[2]
        prev_av_all = None
        prev_out_ok = None

        for bh in range(8):
            s2 = bh % 2
            b_, hl_ = bh >> 1, bh & 1

            def p4_load(sync, bh=bh, s2=s2, b_=b_, hl_=hl_):
                if bh == 0:
                    wt(sync, cc_qk)
                    wt(sync, cc_v)
                    wt(sync, ffn_mms_done)
                    wt(sync, ffn_all_stored)
                wt(sync, qkv_rel[s2])
                for sblk in range(2):
                    base = (2 * b_ + sblk) * 512 + hl_ * 128
                    inc(sync.dma_start(out=qk_buf[:, s2, 0, sblk * 1024:(sblk + 1) * 1024],
                                       in_=aqk_o.ap()[base:base + 128, :]), "sLD", 16)
                    inc(sync.dma_start(out=qk_buf[:, s2, 1, sblk * 1024:(sblk + 1) * 1024],
                                       in_=aqk_o.ap()[base + 256:base + 384, :]), "sLD", 16)
                for part in range(4):
                    inc(sync.dma_start(
                        out=v_buf[:, s2, part * 4:(part + 1) * 4, :],
                        in_=av_o.ap()[b_ * 2048 + part * 512:b_ * 2048 + (part + 1) * 512,
                                      hl_ * 128:(hl_ + 1) * 128].rearrange(
                            "(j p) d -> p j d", p=128)), "sLD", 16)
            if bh == 0 and DEBUG_DUMP:
                def p4_dump0(sync):
                    wt(sync, cc_qk)
                    wt(sync, cc_v)
                    inc(sync.dma_start(out=dbg["d_aqk_ob"], in_=aqk_o.ap()), "sST", 16)
                blk.sync(p4_dump0)
            blk.sync(p4_load)
            bh_loaded = snap("sLD")

            first_nrm = True
            for qk in range(2):
                for ch in range(4):
                    def p4_sq(vector, s2=s2, qk=qk, ch=ch, first=first_nrm,
                              bh_loaded=bh_loaded):
                        if first:
                            wt(vector, bh_loaded)
                        wt(vector, pnorm_rel)
                        inc(vector.tensor_tensor(
                            sqs[:], qk_buf[:, s2, qk, ch * 512:(ch + 1) * 512],
                            qk_buf[:, s2, qk, ch * 512:(ch + 1) * 512], ALU.mult), "sV")
                    blk.vector(p4_sq)
                    sq_ok = snap("sV")
                    first_nrm = False

                    def p4_nmm(tensor, sq_ok=sq_ok):
                        wt(tensor, sq_ok)
                        wt(tensor, pnorm_rel)
                        inc(tensor.matmul(pb[2][:], onesb_s[:], sqs[:],
                                          start=True, stop=True), "sT")
                    blk.tensor(p4_nmm)
                    nmm_ok = cnt["sT"]

                    def p4_nred(vector, qk=qk, ch=ch, nmm_ok=nmm_ok):
                        vector.wait_ge(sems["sT"], nmm_ok)
                        inc(vector.reduce_max(nrm[:, qk * 4 + ch:qk * 4 + ch + 1],
                                              pb[2][:], axis=AX.X), "sV")
                    blk.vector(p4_nred)
                    pnorm_rel = ("sV", cnt["sV"])

            def p4_bound_v(vector, pnorm_rel=pnorm_rel):
                wt(vector, pnorm_rel)
                sw(vector, vector.reduce_max(nrm[:, 8:9], nrm[:, 0:4], axis=AX.X), "sV")
                sw(vector, vector.reduce_max(nrm[:, 9:10], nrm[:, 4:8], axis=AX.X), "sV")
                inc(vector.tensor_mul(nrm[:, 10:11], nrm[:, 8:9], nrm[:, 9:10]), "sV")
            blk.vector(p4_bound_v)
            bm_ok = snap("sV")

            def p4_bound_s(scalar, bm_ok=bm_ok):
                wt(scalar, bm_ok)
                inc(scalar.activation(nrm[:, 11:12], nrm[:, 10:11], AF.Sqrt,
                                      scale=1.0 / D), "sS")
            blk.scalar(p4_bound_s)
            bs_ok = snap("sS")

            def p4_bound_v2(vector, bs_ok=bs_ok):
                wt(vector, bs_ok)
                inc(vector.tensor_scalar_mul(nrm[:, 11:12], nrm[:, 11:12], -1.01), "sV")
            blk.vector(p4_bound_v2)
            bound_ok = snap("sV")

            for qb in range(4):
                jmax = 4 * (qb + 1)
                exp_cnt_at = {}
                s_done = {}

                def p4_s0(tensor, s2=s2, qb=qb, bound_ok=bound_ok):
                    wt(tensor, bound_ok)
                    wt(tensor, pS_rel[0])
                    inc(tensor.matmul(pb[0][:], qk_buf[:, s2, 1, 0:128],
                                      qk_buf[:, s2, 0, qb * 512:(qb + 1) * 512],
                                      start=True, stop=True), "sT")
                blk.tensor(p4_s0)
                s_done[0] = cnt["sT"]

                for j in range(jmax):
                    diag = j >= 4 * qb
                    if diag:
                        def p4_mask(vector, j=j, qb=qb, sd=s_done[j]):
                            vector.wait_ge(sems["sT"], sd)
                            inc(vector.tensor_tensor(pb[j % 2][:], pb[j % 2][:],
                                                     maskb_s[:, j - 4 * qb, :], ALU.add),
                                "sV")
                        blk.vector(p4_mask)
                        mask_ok = snap("sV")
                    else:
                        mask_ok = None

                    if j >= 2:
                        pt_release = s_guard_av[j - 2]
                    else:
                        pt_release = prev_av_all if prev_av_all else None

                    def p4_exp(scalar, j=j, sd=s_done[j], mask_ok=mask_ok,
                               pt_release=pt_release):
                        scalar.wait_ge(sems["sT"], sd)
                        if mask_ok:
                            wt(scalar, mask_ok)
                        if pt_release is not None:
                            scalar.wait_ge(sems["sT"], pt_release)
                        inc(scalar.activation(pT[:, j % 2, :], pb[j % 2][:], AF.Exp,
                                              bias=nrm[:, 11:12], scale=SCALE_S), "sS")
                    blk.scalar(p4_exp)
                    exp_cnt_at[j] = cnt["sS"]

                    def p4_pe_j(tensor, j=j, s2=s2, qb=qb, jmax=jmax,
                                exp_j=exp_cnt_at[j],
                                exp_jm1=exp_cnt_at.get(j - 1),
                                prev_out_ok=prev_out_ok,
                                pS1_prev=(pS_rel[1] if j == 0 else None)):
                        if j + 1 < jmax:
                            if exp_jm1:
                                tensor.wait_ge(sems["sS"], exp_jm1)
                            if pS1_prev is not None:
                                wt(tensor, pS1_prev)
                            inc(tensor.matmul(pb[(j + 1) % 2][:],
                                              qk_buf[:, s2, 1, (j + 1) * 128:(j + 2) * 128],
                                              qk_buf[:, s2, 0, qb * 512:(qb + 1) * 512],
                                              start=True, stop=True), "sT")
                        tensor.wait_ge(sems["sS"], exp_j)
                        if j == 0 and prev_out_ok is not None:
                            tensor.wait_ge(sems["sV"], prev_out_ok)
                        tensor.matmul(pb[4][:], v_buf[:, s2, j, :], pT[:, j % 2, :],
                                      start=(j == 0), stop=(j == jmax - 1),
                                      skip_group_check=True)
                        inc(tensor.matmul(pb[5][:], onesb_s[:], pT[:, j % 2, :],
                                          start=(j == 0), stop=(j == jmax - 1),
                                          skip_group_check=True), "sT")
                    if j == 0:
                        s_guard_av = {}
                    blk.tensor(p4_pe_j)
                    if j + 1 < jmax:
                        s_done[j + 1] = cnt["sT"] - 1
                    s_guard_av[j] = cnt["sT"]

                prev_av_all = cnt["sT"]
                pS_rel = [("sS", exp_cnt_at[jmax - 2]), ("sS", exp_cnt_at[jmax - 1])]

                def p4_out(vector, qb=qb, av_all=prev_av_all):
                    vector.wait_ge(sems["sT"], av_all)
                    wt(vector, stav_rel[qb % 2])
                    sw(vector, vector.reciprocal(st_rc[:, qb % 2, :], pb[5][:]), "sV")
                    inc(vector.tensor_tensor(st_av[:, qb % 2, :], pb[4][:],
                                             st_rc[:, qb % 2, :], ALU.mult), "sV")
                blk.vector(p4_out)
                prev_out_ok = cnt["sV"]
                out_ok = snap("sV")

                def p4_st(sync, b_=b_, hl_=hl_, qb=qb, out_ok=out_ok):
                    wt(sync, out_ok)
                    r = 2 * b_ + qb // 2
                    inc(sync.dma_start(
                        out=aav_i.ap()[r * 256 + hl_ * 128:r * 256 + (hl_ + 1) * 128,
                                       (qb % 2) * 512:(qb % 2) * 512 + 512],
                        in_=st_av[:, qb % 2, :]), "sST", 16)
                blk.sync(p4_st)
                stav_rel[qb % 2] = ("sST", cnt["sST"])
            qkv_rel[s2] = ("sT", cnt["sT"])
        if PHASES < 6:
            def fin0(sync):
                sync.wait_ge(sems["sST"], cnt["sST"])
                sync.wait_ge(sems["sCC"], cnt["sCC"])
            blk.sync(fin0)
            es3.close()
            return nc
        att_all_stored = snap("sST")
        att_done_s = snap("sS")
        att_done_v = snap("sV")
        es3.close()

        def p5_coll(gpsimd):
            wt(gpsimd, att_all_stored)
            with allow_ici_all_to_all():
                inc(gpsimd.collective_compute(
                    "AllToAll", ALU.bypass, replica_groups=[list(range(W))],
                    ins=[aav_i.ap().opt()], outs=[aav_o.ap().opt()]), "sCC")
            inc(gpsimd.collective_compute(
                "AllGather", ALU.bypass, replica_groups=[list(range(W))],
                ins=[bar2_i.ap().opt()], outs=[bar2_o.ap().opt()]), "sCC")
        blk.gpsimd(p5_coll)
        cc_av = ("sCC", 5)

        es4 = ExitStack()
        wout_b = es4.enter_context(nc.sbuf_tensor("wout_b", [128, CCH, 2048], BF16))
        xep = es4.enter_context(nc.sbuf_tensor("xep", [128, 2, 2048], F32))
        fep = es4.enter_context(nc.sbuf_tensor("fep", [128, 2, 2048], F16))
        oep = es4.enter_context(nc.sbuf_tensor("oep", [128, 2, 2048], F32))

        def p5_w(gpsimd):
            wt(gpsimd, cc_av)
            for cc in range(CCH):
                inc(gpsimd.dma_start(out=wout_b[:, cc, :], in_=wout[cc]), "sGW", 16)
            for cc in range(CCH):
                inc(gpsimd.dma_start(out=settle_s[:, 0:16],
                                     in_=wout_b[:, cc, 2032:2048]), "sGW", 16)
        blk.gpsimd(p5_w)
        wout_ok = snap("sGW")

        def p5_rep(sync):
            wt(sync, ffn_all_stored)
            inc(sync.dma_start(out=rep1[:], in_=outb_rep), "sLD", 16)
        blk.sync(p5_rep)
        outb_ok = snap("sLD")

        def p5_at(sync):
            wt(sync, cc_av)
            wt(sync, ffn_all_stored)
            for g in range(CCH):
                inc(sync.dma_start(out=qT_ffn[:, g, :],
                                   in_=aav_o.ap()[g * 128:(g + 1) * 128, :]), "sLD", 16)
        blk.sync(p5_at)
        at_ok = snap("sLD")

        po_rel = [None] * 4
        ep_rel = [None, None]
        for tt in range(NT):
            def p5_ld(sync, tt=tt):
                wt(sync, ep_rel[tt % 2])
                inc(sync.dma_start(out=xep[:, tt % 2, :],
                                   in_=x_own[tt * 128:(tt + 1) * 128, :]), "sLD", 16)
                inc(sync.dma_start(out=fep[:, tt % 2, :],
                                   in_=ffn16d.ap()[tt * 128:(tt + 1) * 128, :]), "sLD", 16)
            blk.sync(p5_ld)
            ep_ld = snap("sLD")
            oep_free = ep_rel[tt % 2]

            for fb in range(4):
                g4 = fb % 4

                def p5_mm(tensor, tt=tt, fb=fb, g4=g4):
                    if tt == 0 and fb == 0:
                        wt(tensor, wout_ok)
                        wt(tensor, at_ok)
                        wt(tensor, att_done_s)
                        wt(tensor, att_done_v)
                    wt(tensor, po_rel[g4])
                    for g in range(CCH):
                        ins = tensor.matmul(pb[g4][:], qT_ffn[:, g, tt * 128:(tt + 1) * 128],
                                            wout_b[:, g, fb * 512:(fb + 1) * 512],
                                            start=(g == 0), stop=(g == CCH - 1))
                    inc(ins, "sT")
                blk.tensor(p5_mm)
                mmo = cnt["sT"]

                def p5_ep(vector, tt=tt, fb=fb, g4=g4, mmo=mmo, ep_ld=ep_ld,
                          oep_free=oep_free):
                    if tt == 0 and fb == 0:
                        wt(vector, outb_ok)
                    if fb == 0:
                        wt(vector, oep_free)
                    vector.wait_ge(sems["sT"], mmo)
                    wt(vector, ep_ld)
                    sl = slice(fb * 512, (fb + 1) * 512)
                    vector.tensor_tensor(oep[:, tt % 2, sl], pb[g4][:], rep1[:, sl], ALU.add)
                    vector.tensor_tensor(oep[:, tt % 2, sl], oep[:, tt % 2, sl],
                                         xep[:, tt % 2, sl], ALU.add)
                    inc(vector.tensor_tensor(oep[:, tt % 2, sl], oep[:, tt % 2, sl],
                                             fep[:, tt % 2, sl], ALU.add), "sV")
                blk.vector(p5_ep)
                po_rel[g4] = ("sV", cnt["sV"])
            ep_done = snap("sV")

            def p5_out(sync, tt=tt, ep_done=ep_done):
                wt(sync, ep_done)
                inc(sync.dma_start(out=y[tt * 128:(tt + 1) * 128, :],
                                   in_=oep[:, tt % 2, :]), "sST", 16)
            blk.sync(p5_out)
            ep_rel[tt % 2] = ("sST", cnt["sST"])

        def fin(sync):
            sync.wait_ge(sems["sST"], cnt["sST"])
            sync.wait_ge(sems["sCC"], cnt["sCC"])
            if DEBUG_DUMP:
                for nm, srct in [("d_aqk_i", aqk_i), ("d_aqk_o", aqk_o), ("d_av_i", av_i),
                                 ("d_av_o", av_o), ("d_aav_i", aav_i), ("d_aav_o", aav_o),
                                 ("d_ffn16", ffn16d)]:
                    inc(sync.dma_start(out=dbg[nm], in_=srct.ap()), "sST", 16)
                sync.wait_ge(sems["sST"], cnt["sST"])
        blk.sync(fin)
        es4.close()

    return nc


_NC_CACHE = {}


def _host_prep(L0_attn_x, L0_attn_qkv_w, L0_attn_qkv_b, L0_attn_out_w,
               L0_attn_out_b, L0_ffn_weight, L0_ffn_w_scale, L0_ffn_bias):
    X = np.ascontiguousarray(np.asarray(L0_attn_x, np.float32).reshape(TOK, C))
    qkv_w = np.asarray(L0_attn_qkv_w, np.float32)
    qkv_b = np.asarray(L0_attn_qkv_b, np.float32)
    out_w = np.asarray(L0_attn_out_w, np.float32)
    out_b = np.asarray(L0_attn_out_b, np.float32)
    fw = np.asarray(L0_ffn_weight, np.int8)
    wsc = np.asarray(L0_ffn_w_scale, np.float32)[:, 0]
    fb = np.asarray(L0_ffn_bias, np.float32)

    rows, brows = [], []
    for r in range(W):
        for base in (0, 2048):
            for hh in (2 * r, 2 * r + 1):
                rows.append(qkv_w[base + hh * 128: base + (hh + 1) * 128])
                brows.append(qkv_b[base + hh * 128: base + (hh + 1) * 128])
    Wsel = np.concatenate(rows, axis=0)
    bsel = np.concatenate(brows, axis=0)
    wqk = np.ascontiguousarray(Wsel.T.reshape(CCH, 128, 32, 128).transpose(2, 1, 0, 3))
    bqk = np.ascontiguousarray(bsel.reshape(32, 128).T)

    Wv = qkv_w[4096:6144]
    wv = np.ascontiguousarray(Wv.T.reshape(CCH, 128, 4, 512).transpose(2, 1, 0, 3))
    vb_rep = np.ascontiguousarray(np.broadcast_to(qkv_b[4096:6144], (128, 2048)))

    wffn = np.ascontiguousarray(fw.T.reshape(CCH, 128, 2048))
    wsc_rep = np.ascontiguousarray(np.broadcast_to(wsc, (128, 2048)))
    fb_rep = np.ascontiguousarray(np.broadcast_to(fb, (128, 2048)))
    wout = np.ascontiguousarray(out_w.T.reshape(CCH, 128, 2048))
    outb_rep = np.ascontiguousarray(np.broadcast_to(out_b, (128, 2048)))

    tpos = np.arange(T, dtype=np.float64)
    freqs = np.arange(0, D, 2, dtype=np.float64) / D
    ang = tpos[:, None] * freqs[None, :]
    emb = np.concatenate([np.sin(ang), np.cos(ang)], axis=1).astype(np.float32)
    embT = np.ascontiguousarray(emb.T)

    jj = np.arange(128)[:, None]
    qq = np.arange(512)[None, :]
    mask = np.zeros((128, 4, 512), np.float32)
    for v in range(4):
        mask[:, v, :] = np.where(qq >= jj + v * 128, 0.0, NEGBIG)
    identf = np.eye(128, dtype=np.float32)
    identb = np.eye(128, dtype=np.float32).astype(ml_dtypes.bfloat16)
    onesb = np.ones((128, 128), np.float32).astype(ml_dtypes.bfloat16)

    kconst = np.zeros((128, 2), np.float32)
    kconst[:, 0] = 1e-5
    shared = dict(wqk=wqk, bqk=bqk, wv=wv, vb_rep=vb_rep, wffn=wffn, kconst=kconst,
                  wsc_rep=wsc_rep, fb_rep=fb_rep, wout=wout, outb_rep=outb_rep,
                  maskb=mask, identf=identf, identb=identb, onesb=onesb)
    in_maps = []
    for c in range(W):
        m = dict(shared)
        m["x_own"] = np.ascontiguousarray(X[c * TOWN:(c + 1) * TOWN])
        pos0 = (c % 2) * TOWN
        m["embt"] = np.ascontiguousarray(embT[:, pos0:pos0 + TOWN])
        in_maps.append(m)
    return in_maps


def kernel(L0_attn_x, L0_attn_qkv_w, L0_attn_qkv_b, L0_attn_out_w, L0_attn_out_b,
           L0_ffn_weight, L0_ffn_w_scale, L0_ffn_bias):
    in_maps = _host_prep(L0_attn_x, L0_attn_qkv_w, L0_attn_qkv_b, L0_attn_out_w,
                         L0_attn_out_b, L0_ffn_weight, L0_ffn_w_scale, L0_ffn_bias)
    if "nc" not in _NC_CACHE:
        _NC_CACHE["nc"] = build()
    nc = _NC_CACHE["nc"]
    res = run_bass_kernel_spmd(nc, in_maps, core_ids=list(range(W)))
    out = np.concatenate([np.asarray(r["y"]) for r in res.results], axis=0)
    return out.reshape(B, T, C)


# version 24
# speedup vs baseline: 1.1484x; 1.0048x over previous
import numpy as np
import ml_dtypes
from contextlib import ExitStack

import concourse.bass as bass
import concourse.mybir as mybir
from concourse.bass_utils import run_bass_kernel_spmd
from concourse.aot_env import allow_ici_all_to_all

F32 = mybir.dt.float32
BF16 = mybir.dt.bfloat16
F16 = mybir.dt.float16
I8 = mybir.dt.int8
I32 = mybir.dt.int32
AF = mybir.ActivationFunctionType
ALU = mybir.AluOpType
AX = mybir.AxisListType

B, T, C, D = 4, 2048, 2048, 128
W = 8
TOK = B * T
TOWN = TOK // W
NT = TOWN // 128
CCH = C // 128
SCALE_S = float(1.0 / np.sqrt(D))
THETA = 0.4999999701976776
NEGBIG = -1.0e30
PHASES = 99
P1_QUANT = True
P1_TP = True
DEBUG_DUMP = False


def build():
    nc = bass.Bass("TRN2", target_bir_lowering=False, detect_race_conditions=False)

    x_own = nc.dram_tensor("x_own", [TOWN, C], F32, kind="ExternalInput").ap()
    wqk = nc.dram_tensor("wqk", [32, 128, CCH, 128], F32, kind="ExternalInput").ap()
    bqk = nc.dram_tensor("bqk", [128, 32], F32, kind="ExternalInput").ap()
    wv = nc.dram_tensor("wv", [4, 128, CCH, 512], F32, kind="ExternalInput").ap()
    vb_rep = nc.dram_tensor("vb_rep", [128, 2048], F32, kind="ExternalInput").ap()
    wffn = nc.dram_tensor("wffn", [CCH, 128, 2048], I8, kind="ExternalInput").ap()
    wsc_rep = nc.dram_tensor("wsc_rep", [128, 2048], F32, kind="ExternalInput").ap()
    fb_rep = nc.dram_tensor("fb_rep", [128, 2048], F32, kind="ExternalInput").ap()
    wout = nc.dram_tensor("wout", [CCH, 128, 2048], F32, kind="ExternalInput").ap()
    outb_rep = nc.dram_tensor("outb_rep", [128, 2048], F32, kind="ExternalInput").ap()
    embt = nc.dram_tensor("embt", [128, TOWN], F32, kind="ExternalInput").ap()
    maskb = nc.dram_tensor("maskb", [128, 4, 512], F32, kind="ExternalInput").ap()
    identf = nc.dram_tensor("identf", [128, 128], F32, kind="ExternalInput").ap()
    identb = nc.dram_tensor("identb", [128, 128], BF16, kind="ExternalInput").ap()
    onesb = nc.dram_tensor("onesb", [128, 128], BF16, kind="ExternalInput").ap()
    kconst = nc.dram_tensor("kconst", [128, 2], F32, kind="ExternalInput").ap()

    y = nc.dram_tensor("y", [TOWN, C], F32, kind="ExternalOutput").ap()

    ffn16d = nc.dram_tensor("ffn16d", [TOWN, C], F16)
    PAD = [4 * 1024 * 1024]
    nc.dram_tensor("pad_a", PAD, F32)
    aqk_i = nc.dram_tensor("aqk_i", [4096, TOWN], BF16)
    nc.dram_tensor("pad_b", PAD, F32)
    aqk_o = nc.dram_tensor("aqk_o", [4096, TOWN], BF16)
    nc.dram_tensor("pad_c", PAD, F32)
    av_i = nc.dram_tensor("av_i", [TOK, 256], BF16)
    nc.dram_tensor("pad_d", PAD, F32)
    av_o = nc.dram_tensor("av_o", [TOK, 256], BF16)
    nc.dram_tensor("pad_e", PAD, F32)
    aav_i = nc.dram_tensor("aav_i", [2048, TOWN], BF16)
    nc.dram_tensor("pad_f", PAD, F32)
    aav_o = nc.dram_tensor("aav_o", [2048, TOWN], BF16)
    nc.dram_tensor("pad_g", PAD, F32)
    bar1_i = nc.dram_tensor("bar1_i", [16, 64], F32)
    nc.dram_tensor("pad_h", PAD, F32)
    bar1_o = nc.dram_tensor("bar1_o", [128, 64], F32)
    nc.dram_tensor("pad_i", PAD, F32)
    bar2_i = nc.dram_tensor("bar2_i", [16, 64], F32)
    nc.dram_tensor("pad_j", PAD, F32)
    bar2_o = nc.dram_tensor("bar2_o", [128, 64], F32)
    nc.dram_tensor("pad_k", PAD, F32)
    dbg = {}
    if DEBUG_DUMP:
        for nm, shape, dt in [("d_aqk_i", [4096, TOWN], BF16), ("d_aqk_o", [4096, TOWN], BF16),
                              ("d_av_i", [TOK, 256], BF16), ("d_av_o", [TOK, 256], BF16),
                              ("d_aav_i", [2048, TOWN], BF16), ("d_aav_o", [2048, TOWN], BF16),
                              ("d_ffn16", [TOWN, C], F16),
                              ("d_xnT", [CCH, 128, TOWN], BF16),
                              ("d_qT", [CCH, 128, TOWN], BF16),
                              ("d_qst", [128, 2048], F32),
                              ("d_vq", [128, 2048], F32),
                              ("d_tq", [128, 2048], I32),
                              ("d_dq", [128, 2048], F32),
                              ("d_xn7", [128, 2048], F32),
                              ("d_aqk_pre", [4096, TOWN], BF16),
                              ("d_ffn16b", [TOWN, C], F16),
                              ("d_aqk_ob", [4096, TOWN], BF16)]:
            dbg[nm] = nc.dram_tensor(nm, shape, dt, kind="ExternalOutput").ap()

    es = ExitStack()
    sb = lambda name, shape, dt: es.enter_context(nc.sbuf_tensor(name, shape, dt))

    embt_s = sb("embt_s", [128, TOWN], F32)
    bqk_s = sb("bqk_s", [128, 32], F32)
    maskb_s = sb("maskb_s", [128, 4, 512], F32)
    identf_s = sb("identf_s", [128, 128], F32)
    identb_s = sb("identb_s", [128, 128], BF16)
    onesb_s = sb("onesb_s", [128, 128], BF16)
    insc = sb("insc", [128, NT], F32)
    rep0 = sb("rep0", [128, 2048], F32)
    rep1 = sb("rep1", [128, 2048], F32)
    qT_ffn = sb("qT_ffn", [128, CCH, TOWN], BF16)
    st_qk = sb("st_qk", [128, 2, 512], BF16)
    st_v = sb("st_v", [128, 2, 512], BF16)
    nrm = sb("nrm", [128, 12], F32)
    ln1 = sb("ln1", [128, 16], F32)
    kconst_s = sb("kconst_s", [128, 2], F32)
    ev_tmp = sb("ev_tmp", [128, 512], F32)
    settle_s = sb("settle_s", [128, 32], BF16)

    pb = [es.enter_context(nc.psum_tensor(f"pb{i}", [128, 512], F32)) for i in range(8)]

    cnt = {}
    sems = {}
    for name in ["sLD", "sGW", "sST", "sV", "sS", "sT", "sCC"]:
        sems[name] = es.enter_context(nc.semaphore(name))
        cnt[name] = 0

    def inc(ins, sem, k=1):
        cnt[sem] += k
        return ins.then_inc(sems[sem], k)

    def snap(sem):
        return (sem, cnt[sem])

    def wt(eng, rel):
        if rel is not None and rel[1] > 0:
            eng.wait_ge(sems[rel[0]], rel[1])

    def sw(eng, ins, sem):
        inc(ins, sem)
        eng.wait_ge(sems[sem], cnt[sem])

    with es, nc.Block() as blk:
        def p0(sync):
            inc(sync.dma_start(out=embt_s[:], in_=embt), "sLD", 16)
            inc(sync.dma_start(out=bqk_s[:], in_=bqk), "sLD", 16)
            inc(sync.dma_start(out=maskb_s[:], in_=maskb), "sLD", 16)
            inc(sync.dma_start(out=identf_s[:], in_=identf), "sLD", 16)
            inc(sync.dma_start(out=identb_s[:], in_=identb), "sLD", 16)
            inc(sync.dma_start(out=onesb_s[:], in_=onesb), "sLD", 16)
            inc(sync.dma_start(out=rep0[:], in_=vb_rep), "sLD", 16)
            inc(sync.dma_start(out=kconst_s[:], in_=kconst), "sLD", 16)
        blk.sync(p0)
        consts_ok = snap("sLD")

        es1 = ExitStack()
        sb1 = lambda n, s, d: es1.enter_context(nc.sbuf_tensor(n, s, d))
        xst = sb1("xst", [128, 2, 2048], F32)
        xnst = sb1("xnst", [128, 2, 2048], F32)
        qst = sb1("qst", [128, 2048], F32)
        xnT = sb1("xnT", [128, CCH, TOWN], BF16)
        wqk_b = sb1("wqk_b", [128, 2, CCH, 128], BF16)
        wv_b = sb1("wv_b", [128, 2, CCH, 512], BF16)
        es1a = ExitStack()
        sb1a = lambda n, s, d: es1a.enter_context(nc.sbuf_tensor(n, s, d))
        vq_s = sb1a("vq_s", [128, 2048], F32)
        tq_s = sb1a("tq_s", [128, 2048], I32)
        dq_s = sb1a("dq_s", [128, 2048], F32)

        x_rel = [None, None]
        xn_rel = [None, None]
        tslot_rel = [None, None]

        for i in range(NT):
            s = i % 2
            o = s * 8
            qst_rel = snap("sT")

            def p1_load(sync, i=i, s=s):
                wt(sync, x_rel[s])
                inc(sync.dma_start(out=xst[:, s, :],
                                   in_=x_own[i * 128:(i + 1) * 128, :]), "sLD", 16)
            blk.sync(p1_load)
            x_ready = snap("sLD")

            def p1_dve1(vector, s=s, o=o, x_ready=x_ready):
                wt(vector, x_ready)
                sw(vector, vector.reduce_sum(ln1[:, o + 0:o + 1], xst[:, s, :],
                                             axis=AX.X), "sV")
                inc(vector.tensor_scalar_mul(ln1[:, o + 1:o + 2], ln1[:, o + 0:o + 1],
                                             -1.0 / 2048.0), "sV")
            blk.vector(p1_dve1)
            negm_ok = snap("sV")

            def p1_sc1(scalar, s=s, o=o, negm_ok=negm_ok):
                wt(scalar, negm_ok)
                sw(scalar, scalar.activation(dq_s[:], xst[:, s, :], AF.Square,
                                  bias=ln1[:, o + 1:o + 2],
                                  accum_out=ln1[:, o + 2:o + 3]), "sS")
                inc(scalar.activation(ln1[:, o + 3:o + 4], ln1[:, o + 2:o + 3], AF.Sqrt,
                                      bias=kconst_s[:, 0:1], scale=1.0 / 2048.0), "sS")
            blk.scalar(p1_sc1)
            sd_ok = snap("sS")

            def p1_dve2(vector, i=i, s=s, o=o, sd_ok=sd_ok, qst_rel=qst_rel):
                wt(vector, sd_ok)
                sw(vector, vector.reciprocal(ln1[:, o + 4:o + 5], ln1[:, o + 3:o + 4]),
                   "sV")
                wt(vector, xn_rel[s])
                inc(vector.tensor_scalar(xnst[:, s, :], xst[:, s, :], ln1[:, o + 1:o + 2],
                                         ln1[:, o + 4:o + 5], ALU.add, ALU.mult), "sV")
                sw(vector, vector.reduce_max(ln1[:, o + 5:o + 6], xnst[:, s, :],
                                             axis=AX.X, apply_absolute_value=True), "sV")
                sw(vector, vector.tensor_scalar_max(ln1[:, o + 6:o + 7],
                                                    ln1[:, o + 5:o + 6], 1e-7), "sV")
                vector.tensor_scalar_mul(insc[:, i:i + 1], ln1[:, o + 6:o + 7], 1.0 / 127.0)
                if not P1_QUANT:
                    inc(vector.tensor_copy(qst[:], xnst[:, s, :]), "sV")
                    return
                sw(vector, vector.reciprocal(ln1[:, o + 7:o + 8], ln1[:, o + 6:o + 7]),
                   "sV")
                sw(vector, vector.tensor_scalar_mul(ln1[:, o + 7:o + 8],
                                                    ln1[:, o + 7:o + 8], -127.0), "sV")
                vector.tensor_scalar_mul(vq_s[:], xnst[:, s, :], ln1[:, o + 7:o + 8])
                vector.tensor_copy(tq_s[:], vq_s[:])
                vector.tensor_sub(dq_s[:], vq_s[:], tq_s[:])
                vector.scalar_tensor_tensor(vq_s[:], dq_s[:], THETA, tq_s[:],
                                            ALU.is_gt, ALU.add)
                wt(vector, qst_rel)
                inc(vector.scalar_tensor_tensor(qst[:], dq_s[:], -THETA, vq_s[:],
                                                ALU.is_lt, ALU.subtract), "sV")
            blk.vector(p1_dve2)
            xn_ok = snap("sV")

            vbase = cnt["sV"]
            tslot_prev = list(tslot_rel)
            t0 = cnt["sT"]
            if not P1_TP:
                x_rel[s] = xn_ok
                xn_rel[s] = xn_ok
                continue

            def p1_pe(tensor, s=s, vbase=vbase, tslot_prev=tslot_prev, xn_ok=xn_ok):
                wt(tensor, consts_ok)
                wt(tensor, xn_ok)
                for g in range(8):
                    src = xnst[:, s, :] if g < 4 else qst[:]
                    jb = (g % 4) * 4
                    bank = pb[6 + g % 2]
                    if g < 2:
                        wt(tensor, tslot_prev[g % 2])
                    else:
                        tensor.wait_ge(sems["sV"], vbase + g - 1)
                    for k in range(4):
                        ins = tensor.transpose(bank[:, k * 128:(k + 1) * 128],
                                               src[:, (jb + k) * 128:(jb + k + 1) * 128],
                                               identf_s[:])
                    inc(ins, "sT")
            blk.tensor(p1_pe)

            def p1_evac(vector, i=i, t0=t0):
                for g in range(8):
                    jb = (g % 4) * 4
                    bank = pb[6 + g % 2].ap().rearrange("p (a b) -> p a b", a=4)
                    dst = xnT if g < 4 else qT_ffn
                    vector.wait_ge(sems["sT"], t0 + g + 1)
                    inc(vector.tensor_copy(dst[:, jb:jb + 4, i * 128:(i + 1) * 128],
                                           bank), "sV")
            blk.vector(p1_evac)
            tslot_rel = [("sV", vbase + 7), ("sV", vbase + 8)]
            x_rel[s] = xn_ok
            xn_rel[s] = snap("sV")

        p1_done_v = snap("sV")
        if DEBUG_DUMP:
            def p1_dump(sync):
                wt(sync, p1_done_v)
                inc(sync.dma_start(out=dbg["d_qst"], in_=qst[:]), "sST", 16)
                inc(sync.dma_start(out=dbg["d_vq"], in_=vq_s[:]), "sST", 16)
                inc(sync.dma_start(out=dbg["d_tq"], in_=tq_s[:]), "sST", 16)
                inc(sync.dma_start(out=dbg["d_dq"], in_=dq_s[:]), "sST", 16)
                inc(sync.dma_start(out=dbg["d_xn7"], in_=xnst[:, 1, :]), "sST", 16)
                inc(sync.dma_start(out=dbg["d_xnT"].rearrange("a p b -> p a b"),
                                   in_=xnT[:]), "sST", 16)
                inc(sync.dma_start(out=dbg["d_qT"].rearrange("a p b -> p a b"),
                                   in_=qT_ffn[:]), "sST", 16)
            blk.sync(p1_dump)
            def p1_dump_wait(sync):
                sync.wait_ge(sems["sST"], cnt["sST"])
            blk.sync(p1_dump_wait)
        es1a.close()
        if PHASES < 2:
            es1.close()
            def fin0(sync):
                sync.wait_ge(sems["sV"], cnt["sV"])
            blk.sync(fin0)
            return nc

        T0 = cnt["sT"]
        T0v = T0 + 64
        gw_base = cnt["sGW"]

        def p2_w(gpsimd):
            for fc in range(32):
                if fc >= 2:
                    gpsimd.wait_ge(sems["sT"], T0 + 2 * (fc - 1))
                inc(gpsimd.dma_start(out=wqk_b[:, fc % 2, :, :], in_=wqk[fc]), "sGW", 16)
                inc(gpsimd.dma_start(out=settle_s[:, 0:16],
                                     in_=wqk_b[:, fc % 2, CCH - 1, 112:128]), "sGW", 16)
            for rp in range(4):
                if rp >= 2:
                    gpsimd.wait_ge(sems["sT"], T0v + (rp - 1) * 8)
                inc(gpsimd.dma_start(out=wv_b[:, rp % 2, :, :], in_=wv[rp]), "sGW", 16)
                inc(gpsimd.dma_start(out=settle_s[:, 16:32],
                                     in_=wv_b[:, rp % 2, CCH - 1, 496:512]), "sGW", 16)
        blk.gpsimd(p2_w)

        pqk_rel = [None, None]
        stqk_rel = [None, None]
        for fc in range(32):
            def p2_mm(tensor, fc=fc):
                if fc == 0:
                    wt(tensor, p1_done_v)
                tensor.wait_ge(sems["sGW"], gw_base + (fc + 1) * 32)
                for th in range(2):
                    wt(tensor, pqk_rel[th])
                    for cc in range(CCH):
                        ins = tensor.matmul(pb[th][:], wqk_b[:, fc % 2, cc, :],
                                            xnT[:, cc, th * 512:(th + 1) * 512],
                                            start=(cc == 0), stop=(cc == CCH - 1))
                    inc(ins, "sT")
            blk.tensor(p2_mm)
            mm_now = cnt["sT"]
            assert mm_now == T0 + 2 * (fc + 1)

            def p2_ev(vector, fc=fc, mm_now=mm_now):
                if fc == 0:
                    wt(vector, consts_ok)
                for th in range(2):
                    vector.wait_ge(sems["sT"], mm_now - 1 + th)
                    wt(vector, stqk_rel[th])
                    vector.tensor_scalar_add(ev_tmp[:], pb[th][:],
                                             bqk_s[:, fc:fc + 1])
                    inc(vector.tensor_tensor(
                        st_qk[:, th, :], ev_tmp[:],
                        embt_s[:, th * 512:(th + 1) * 512], ALU.mult), "sV")
            blk.vector(p2_ev)
            ev_now = cnt["sV"]
            pqk_rel = [("sV", ev_now - 1), ("sV", ev_now)]

            def p2_st(sync, fc=fc, ev_now=ev_now):
                for th in range(2):
                    sync.wait_ge(sems["sV"], ev_now - 1 + th)
                    inc(sync.dma_start(
                        out=aqk_i.ap()[fc * 128:(fc + 1) * 128, th * 512:(th + 1) * 512],
                        in_=st_qk[:, th, :]), "sST", 16)
            blk.sync(p2_st)
            stqk_rel = [("sST", cnt["sST"] - 16), ("sST", cnt["sST"])]
        qk_all_stored = snap("sST")
        if DEBUG_DUMP:
            def p2_dump(sync):
                wt(sync, qk_all_stored)
                inc(sync.dma_start(out=dbg["d_aqk_pre"], in_=aqk_i.ap()), "sST", 16)
            blk.sync(p2_dump)
            qk_all_stored = snap("sST")

        pv_rel = [None, None]
        stv_rel = [None, None]
        for rp in range(4):
            for tt in range(NT):
                def p2v_mm(tensor, rp=rp, tt=tt):
                    if tt == 0:
                        tensor.wait_ge(sems["sGW"], gw_base + 32 * 32 + (rp + 1) * 32)
                    wt(tensor, pv_rel[tt % 2])
                    for cc in range(CCH):
                        ins = tensor.matmul(pb[2 + tt % 2][:],
                                            xnT[:, cc, tt * 128:(tt + 1) * 128],
                                            wv_b[:, rp % 2, cc, :],
                                            start=(cc == 0), stop=(cc == CCH - 1))
                    inc(ins, "sT")
                blk.tensor(p2v_mm)
                mmv = cnt["sT"]
                assert mmv == T0v + rp * 8 + tt + 1

                def p2v_ev(vector, rp=rp, tt=tt, mmv=mmv):
                    vector.wait_ge(sems["sT"], mmv)
                    wt(vector, stv_rel[tt % 2])
                    inc(vector.tensor_tensor(st_v[:, tt % 2, :], pb[2 + tt % 2][:],
                                             rep0[:, rp * 512:(rp + 1) * 512], ALU.add), "sV")
                blk.vector(p2v_ev)
                evv = cnt["sV"]
                pv_rel[tt % 2] = ("sV", evv)

                def p2v_st(sync, rp=rp, tt=tt, evv=evv):
                    sync.wait_ge(sems["sV"], evv)
                    r0 = 2 * rp
                    inc(sync.dma_start(
                        out=av_i.ap()[r0 * 1024 + tt * 128:r0 * 1024 + (tt + 1) * 128, :],
                        in_=st_v[:, tt % 2, 0:256]), "sST", 16)
                    inc(sync.dma_start(
                        out=av_i.ap()[(r0 + 1) * 1024 + tt * 128:(r0 + 1) * 1024 + (tt + 1) * 128, :],
                        in_=st_v[:, tt % 2, 256:512]), "sST", 16)
                blk.sync(p2v_st)
                stv_rel[tt % 2] = ("sST", cnt["sST"])
        v_all_stored = snap("sST")
        p2_evacs_done = snap("sV")
        es1.close()
        if PHASES < 3:
            def fin0(sync):
                sync.wait_ge(sems["sST"], cnt["sST"])
            blk.sync(fin0)
            return nc

        es2 = ExitStack()
        wffn_b = es2.enter_context(nc.sbuf_tensor("wffn_b", [128, CCH, 2048], BF16))
        st_f1 = es2.enter_context(nc.sbuf_tensor("st_f1", [128, 2, 512], F32))
        st_f2 = es2.enter_context(nc.sbuf_tensor("st_f2", [128, 2, 512], F16))

        def p3_w(gpsimd):
            wt(gpsimd, v_all_stored)
            for cc in range(CCH):
                inc(gpsimd.dma_start(out=wffn_b[:, cc, :], in_=wffn[cc]), "sGW", 16)
            for cc in range(CCH):
                inc(gpsimd.dma_start(out=settle_s[:, 0:16],
                                     in_=wffn_b[:, cc, 2032:2048]), "sGW", 16)
        blk.gpsimd(p3_w)
        wffn_ok = snap("sGW")

        def p3_rep(sync):
            wt(sync, v_all_stored)
            inc(sync.dma_start(out=rep0[:], in_=wsc_rep), "sLD", 16)
            inc(sync.dma_start(out=rep1[:], in_=fb_rep), "sLD", 16)
        blk.sync(p3_rep)
        reps_ok = snap("sLD")

        pf_rel = [p2_evacs_done] * 4
        stf_rel = [None, None]
        for tt in range(NT):
            for nb in range(4):
                g = (tt * 4 + nb) % 4

                def p3_mm(tensor, tt=tt, nb=nb, g=g):
                    if tt == 0 and nb == 0:
                        wt(tensor, wffn_ok)
                    wt(tensor, pf_rel[g])
                    for cc in range(CCH):
                        ins = tensor.matmul(pb[g][:], qT_ffn[:, cc, tt * 128:(tt + 1) * 128],
                                            wffn_b[:, cc, nb * 512:(nb + 1) * 512],
                                            start=(cc == 0), stop=(cc == CCH - 1))
                    inc(ins, "sT")
                blk.tensor(p3_mm)
                mmf = cnt["sT"]

                def p3_ev(vector, tt=tt, nb=nb, g=g, mmf=mmf):
                    if tt == 0 and nb == 0:
                        wt(vector, reps_ok)
                    vector.wait_ge(sems["sT"], mmf)
                    wt(vector, stf_rel[nb % 2])
                    vector.tensor_scalar_mul(ev_tmp[:], pb[g][:], insc[:, tt:tt + 1])
                    vector.tensor_tensor(st_f1[:, nb % 2, :], ev_tmp[:],
                                         rep0[:, nb * 512:(nb + 1) * 512], ALU.mult)
                    inc(vector.tensor_tensor(st_f2[:, nb % 2, :], st_f1[:, nb % 2, :],
                                             rep1[:, nb * 512:(nb + 1) * 512], ALU.add), "sV")
                blk.vector(p3_ev)
                evf = cnt["sV"]
                pf_rel[g] = ("sV", evf)

                def p3_st(sync, tt=tt, nb=nb, evf=evf):
                    sync.wait_ge(sems["sV"], evf)
                    inc(sync.dma_start(
                        out=ffn16d.ap()[tt * 128:(tt + 1) * 128, nb * 512:(nb + 1) * 512],
                        in_=st_f2[:, nb % 2, :]), "sST", 16)
                blk.sync(p3_st)
                stf_rel[nb % 2] = ("sST", cnt["sST"])
        ffn_mms_done = snap("sT")
        if DEBUG_DUMP:
            def p3_dump(sync):
                wt(sync, ffn_all_stored if False else snap("sST"))
                inc(sync.dma_start(out=dbg["d_ffn16b"], in_=ffn16d.ap()), "sST", 16)
            blk.sync(p3_dump)
        es2.close()
        if PHASES < 5:
            def fin0(sync):
                sync.wait_ge(sems["sST"], cnt["sST"])
                sync.wait_ge(sems["sCC"], cnt["sCC"])
            blk.sync(fin0)
            return nc
        ffn_all_stored = snap("sST")
        p3_evacs_done = snap("sV")

        def p_coll(gpsimd):
            wt(gpsimd, qk_all_stored)
            wt(gpsimd, ffn_all_stored)
            wt(gpsimd, ("sT", ffn_mms_done[1]))
            with allow_ici_all_to_all():
                inc(gpsimd.collective_compute(
                    "AllToAll", ALU.bypass, replica_groups=[list(range(W))],
                    ins=[aqk_i.ap().opt()], outs=[aqk_o.ap().opt()]), "sCC")
            wt(gpsimd, v_all_stored)
            with allow_ici_all_to_all():
                inc(gpsimd.collective_compute(
                    "AllToAll", ALU.bypass, replica_groups=[list(range(W))],
                    ins=[av_i.ap().opt()], outs=[av_o.ap().opt()]), "sCC")
            inc(gpsimd.collective_compute(
                "AllGather", ALU.bypass, replica_groups=[list(range(W))],
                ins=[bar1_i.ap().opt()], outs=[bar1_o.ap().opt()]), "sCC")
        blk.gpsimd(p_coll)
        if PHASES < 4:
            def fin0(sync):
                sync.wait_ge(sems["sCC"], cnt["sCC"])
            blk.sync(fin0)
            return nc
        cc_qk = ("sCC", 1)
        cc_v = ("sCC", 3)

        es3 = ExitStack()
        qk_buf = es3.enter_context(nc.sbuf_tensor("qk_buf", [128, 2, 2, 2048], BF16))
        v_buf = es3.enter_context(nc.sbuf_tensor("v_buf", [128, 2, CCH, 128], BF16))
        sqs = es3.enter_context(nc.sbuf_tensor("sqs", [128, 512], BF16))
        pT = es3.enter_context(nc.sbuf_tensor("pT", [128, 3, 512], BF16))
        st_av = es3.enter_context(nc.sbuf_tensor("st_av", [128, 2, 512], BF16))
        st_rc = es3.enter_context(nc.sbuf_tensor("st_rc", [128, 2, 512], F32))

        qkv_rel = [None, None]
        stav_rel = [None, None]
        pS_rel = [pf_rel[0], pf_rel[1]]
        pnorm_rel = pf_rel[2]
        prev_av_all = None
        prev_out_ok = None

        for bh in range(8):
            s2 = bh % 2
            b_, hl_ = bh >> 1, bh & 1

            def p4_load(sync, bh=bh, s2=s2, b_=b_, hl_=hl_):
                if bh == 0:
                    wt(sync, cc_qk)
                    wt(sync, cc_v)
                    wt(sync, ffn_mms_done)
                    wt(sync, ffn_all_stored)
                wt(sync, qkv_rel[s2])
                for sblk in range(2):
                    base = (2 * b_ + sblk) * 512 + hl_ * 128
                    inc(sync.dma_start(out=qk_buf[:, s2, 0, sblk * 1024:(sblk + 1) * 1024],
                                       in_=aqk_o.ap()[base:base + 128, :]), "sLD", 16)
                    inc(sync.dma_start(out=qk_buf[:, s2, 1, sblk * 1024:(sblk + 1) * 1024],
                                       in_=aqk_o.ap()[base + 256:base + 384, :]), "sLD", 16)
                for part in range(4):
                    inc(sync.dma_start(
                        out=v_buf[:, s2, part * 4:(part + 1) * 4, :],
                        in_=av_o.ap()[b_ * 2048 + part * 512:b_ * 2048 + (part + 1) * 512,
                                      hl_ * 128:(hl_ + 1) * 128].rearrange(
                            "(j p) d -> p j d", p=128)), "sLD", 16)
            if bh == 0 and DEBUG_DUMP:
                def p4_dump0(sync):
                    wt(sync, cc_qk)
                    wt(sync, cc_v)
                    inc(sync.dma_start(out=dbg["d_aqk_ob"], in_=aqk_o.ap()), "sST", 16)
                blk.sync(p4_dump0)
            blk.sync(p4_load)
            bh_loaded = snap("sLD")

            first_nrm = True
            for qk in range(2):
                for ch in range(4):
                    def p4_sq(vector, s2=s2, qk=qk, ch=ch, first=first_nrm,
                              bh_loaded=bh_loaded):
                        if first:
                            wt(vector, bh_loaded)
                        wt(vector, pnorm_rel)
                        inc(vector.tensor_tensor(
                            sqs[:], qk_buf[:, s2, qk, ch * 512:(ch + 1) * 512],
                            qk_buf[:, s2, qk, ch * 512:(ch + 1) * 512], ALU.mult), "sV")
                    blk.vector(p4_sq)
                    sq_ok = snap("sV")
                    first_nrm = False

                    def p4_nmm(tensor, sq_ok=sq_ok):
                        wt(tensor, sq_ok)
                        wt(tensor, pnorm_rel)
                        inc(tensor.matmul(pb[2][:], onesb_s[:], sqs[:],
                                          start=True, stop=True), "sT")
                    blk.tensor(p4_nmm)
                    nmm_ok = cnt["sT"]

                    def p4_nred(vector, qk=qk, ch=ch, nmm_ok=nmm_ok):
                        vector.wait_ge(sems["sT"], nmm_ok)
                        inc(vector.reduce_max(nrm[:, qk * 4 + ch:qk * 4 + ch + 1],
                                              pb[2][:], axis=AX.X), "sV")
                    blk.vector(p4_nred)
                    pnorm_rel = ("sV", cnt["sV"])

            def p4_bound_v(vector, pnorm_rel=pnorm_rel):
                wt(vector, pnorm_rel)
                sw(vector, vector.reduce_max(nrm[:, 8:9], nrm[:, 0:4], axis=AX.X), "sV")
                sw(vector, vector.reduce_max(nrm[:, 9:10], nrm[:, 4:8], axis=AX.X), "sV")
                inc(vector.tensor_mul(nrm[:, 10:11], nrm[:, 8:9], nrm[:, 9:10]), "sV")
            blk.vector(p4_bound_v)
            bm_ok = snap("sV")

            def p4_bound_s(scalar, bm_ok=bm_ok):
                wt(scalar, bm_ok)
                inc(scalar.activation(nrm[:, 11:12], nrm[:, 10:11], AF.Sqrt,
                                      scale=1.0 / D), "sS")
            blk.scalar(p4_bound_s)
            bs_ok = snap("sS")

            def p4_bound_v2(vector, bs_ok=bs_ok):
                wt(vector, bs_ok)
                inc(vector.tensor_scalar_mul(nrm[:, 11:12], nrm[:, 11:12], -1.01), "sV")
            blk.vector(p4_bound_v2)
            bound_ok = snap("sV")

            for qb in range(4):
                jmax = 4 * (qb + 1)
                exp_cnt_at = {}
                s_done = {}

                def p4_s0(tensor, s2=s2, qb=qb, bound_ok=bound_ok):
                    wt(tensor, bound_ok)
                    wt(tensor, pS_rel[0])
                    inc(tensor.matmul(pb[0][:], qk_buf[:, s2, 1, 0:128],
                                      qk_buf[:, s2, 0, qb * 512:(qb + 1) * 512],
                                      start=True, stop=True), "sT")
                blk.tensor(p4_s0)
                s_done[0] = cnt["sT"]

                for j in range(jmax):
                    diag = j >= 4 * qb
                    if diag:
                        def p4_mask(vector, j=j, qb=qb, sd=s_done[j]):
                            vector.wait_ge(sems["sT"], sd)
                            inc(vector.tensor_tensor(pb[j % 2][:], pb[j % 2][:],
                                                     maskb_s[:, j - 4 * qb, :], ALU.add),
                                "sV")
                        blk.vector(p4_mask)
                        mask_ok = snap("sV")
                    else:
                        mask_ok = None

                    def p4_exp(scalar, j=j, sd=s_done[j], mask_ok=mask_ok):
                        if mask_ok:
                            wt(scalar, mask_ok)
                        else:
                            scalar.wait_ge(sems["sT"], sd)
                        inc(scalar.activation(pT[:, j % 3, :], pb[j % 2][:], AF.Exp,
                                              bias=nrm[:, 11:12], scale=SCALE_S), "sS")
                    blk.scalar(p4_exp)
                    exp_cnt_at[j] = cnt["sS"]

                    def p4_pe_j(tensor, j=j, s2=s2, qb=qb, jmax=jmax,
                                exp_j=exp_cnt_at[j],
                                exp_jm1=exp_cnt_at.get(j - 1),
                                prev_out_ok=prev_out_ok,
                                pS1_prev=(pS_rel[1] if j == 0 else None)):
                        if j + 1 < jmax:
                            if exp_jm1:
                                tensor.wait_ge(sems["sS"], exp_jm1)
                            if pS1_prev is not None:
                                wt(tensor, pS1_prev)
                            inc(tensor.matmul(pb[(j + 1) % 2][:],
                                              qk_buf[:, s2, 1, (j + 1) * 128:(j + 2) * 128],
                                              qk_buf[:, s2, 0, qb * 512:(qb + 1) * 512],
                                              start=True, stop=True), "sT")
                        tensor.wait_ge(sems["sS"], exp_j)
                        if j == 0 and prev_out_ok is not None:
                            tensor.wait_ge(sems["sV"], prev_out_ok)
                        tensor.matmul(pb[4][:], v_buf[:, s2, j, :], pT[:, j % 3, :],
                                      start=(j == 0), stop=(j == jmax - 1),
                                      skip_group_check=True)
                        inc(tensor.matmul(pb[5][:], onesb_s[:], pT[:, j % 3, :],
                                          start=(j == 0), stop=(j == jmax - 1),
                                          skip_group_check=True), "sT")
                    if j == 0:
                        s_guard_av = {}
                    blk.tensor(p4_pe_j)
                    if j + 1 < jmax:
                        s_done[j + 1] = cnt["sT"] - 1
                    s_guard_av[j] = cnt["sT"]

                prev_av_all = cnt["sT"]
                pS_rel = [("sS", exp_cnt_at[jmax - 2]), ("sS", exp_cnt_at[jmax - 1])]

                def p4_out(vector, qb=qb, av_all=prev_av_all):
                    vector.wait_ge(sems["sT"], av_all)
                    wt(vector, stav_rel[qb % 2])
                    sw(vector, vector.reciprocal(st_rc[:, qb % 2, :], pb[5][:]), "sV")
                    inc(vector.tensor_tensor(st_av[:, qb % 2, :], pb[4][:],
                                             st_rc[:, qb % 2, :], ALU.mult), "sV")
                blk.vector(p4_out)
                prev_out_ok = cnt["sV"]
                out_ok = snap("sV")

                def p4_st(sync, b_=b_, hl_=hl_, qb=qb, out_ok=out_ok):
                    wt(sync, out_ok)
                    r = 2 * b_ + qb // 2
                    inc(sync.dma_start(
                        out=aav_i.ap()[r * 256 + hl_ * 128:r * 256 + (hl_ + 1) * 128,
                                       (qb % 2) * 512:(qb % 2) * 512 + 512],
                        in_=st_av[:, qb % 2, :]), "sST", 16)
                blk.sync(p4_st)
                stav_rel[qb % 2] = ("sST", cnt["sST"])
            qkv_rel[s2] = ("sT", cnt["sT"])
        if PHASES < 6:
            def fin0(sync):
                sync.wait_ge(sems["sST"], cnt["sST"])
                sync.wait_ge(sems["sCC"], cnt["sCC"])
            blk.sync(fin0)
            es3.close()
            return nc
        att_all_stored = snap("sST")
        att_done_s = snap("sS")
        att_done_v = snap("sV")
        es3.close()

        def p5_coll(gpsimd):
            wt(gpsimd, att_all_stored)
            with allow_ici_all_to_all():
                inc(gpsimd.collective_compute(
                    "AllToAll", ALU.bypass, replica_groups=[list(range(W))],
                    ins=[aav_i.ap().opt()], outs=[aav_o.ap().opt()]), "sCC")
            inc(gpsimd.collective_compute(
                "AllGather", ALU.bypass, replica_groups=[list(range(W))],
                ins=[bar2_i.ap().opt()], outs=[bar2_o.ap().opt()]), "sCC")
        blk.gpsimd(p5_coll)
        cc_av = ("sCC", 5)

        es4 = ExitStack()
        wout_b = es4.enter_context(nc.sbuf_tensor("wout_b", [128, CCH, 2048], BF16))
        xep = es4.enter_context(nc.sbuf_tensor("xep", [128, 2, 2048], F32))
        fep = es4.enter_context(nc.sbuf_tensor("fep", [128, 2, 2048], F16))
        oep = es4.enter_context(nc.sbuf_tensor("oep", [128, 2, 2048], F32))

        def p5_w(gpsimd):
            wt(gpsimd, cc_av)
            for cc in range(CCH):
                inc(gpsimd.dma_start(out=wout_b[:, cc, :], in_=wout[cc]), "sGW", 16)
            for cc in range(CCH):
                inc(gpsimd.dma_start(out=settle_s[:, 0:16],
                                     in_=wout_b[:, cc, 2032:2048]), "sGW", 16)
        blk.gpsimd(p5_w)
        wout_ok = snap("sGW")

        def p5_rep(sync):
            wt(sync, ffn_all_stored)
            inc(sync.dma_start(out=rep1[:], in_=outb_rep), "sLD", 16)
        blk.sync(p5_rep)
        outb_ok = snap("sLD")

        def p5_at(sync):
            wt(sync, cc_av)
            wt(sync, ffn_all_stored)
            for g in range(CCH):
                inc(sync.dma_start(out=qT_ffn[:, g, :],
                                   in_=aav_o.ap()[g * 128:(g + 1) * 128, :]), "sLD", 16)
        blk.sync(p5_at)
        at_ok = snap("sLD")

        po_rel = [None] * 4
        ep_rel = [None, None]
        for tt in range(NT):
            def p5_ld(sync, tt=tt):
                wt(sync, ep_rel[tt % 2])
                inc(sync.dma_start(out=xep[:, tt % 2, :],
                                   in_=x_own[tt * 128:(tt + 1) * 128, :]), "sLD", 16)
                inc(sync.dma_start(out=fep[:, tt % 2, :],
                                   in_=ffn16d.ap()[tt * 128:(tt + 1) * 128, :]), "sLD", 16)
            blk.sync(p5_ld)
            ep_ld = snap("sLD")
            oep_free = ep_rel[tt % 2]

            for fb in range(4):
                g4 = fb % 4

                def p5_mm(tensor, tt=tt, fb=fb, g4=g4):
                    if tt == 0 and fb == 0:
                        wt(tensor, wout_ok)
                        wt(tensor, at_ok)
                        wt(tensor, att_done_s)
                        wt(tensor, att_done_v)
                    wt(tensor, po_rel[g4])
                    for g in range(CCH):
                        ins = tensor.matmul(pb[g4][:], qT_ffn[:, g, tt * 128:(tt + 1) * 128],
                                            wout_b[:, g, fb * 512:(fb + 1) * 512],
                                            start=(g == 0), stop=(g == CCH - 1))
                    inc(ins, "sT")
                blk.tensor(p5_mm)
                mmo = cnt["sT"]

                def p5_ep(vector, tt=tt, fb=fb, g4=g4, mmo=mmo, ep_ld=ep_ld,
                          oep_free=oep_free):
                    if tt == 0 and fb == 0:
                        wt(vector, outb_ok)
                    if fb == 0:
                        wt(vector, oep_free)
                    vector.wait_ge(sems["sT"], mmo)
                    wt(vector, ep_ld)
                    sl = slice(fb * 512, (fb + 1) * 512)
                    vector.tensor_tensor(oep[:, tt % 2, sl], pb[g4][:], rep1[:, sl], ALU.add)
                    vector.tensor_tensor(oep[:, tt % 2, sl], oep[:, tt % 2, sl],
                                         xep[:, tt % 2, sl], ALU.add)
                    inc(vector.tensor_tensor(oep[:, tt % 2, sl], oep[:, tt % 2, sl],
                                             fep[:, tt % 2, sl], ALU.add), "sV")
                blk.vector(p5_ep)
                po_rel[g4] = ("sV", cnt["sV"])
            ep_done = snap("sV")

            def p5_out(sync, tt=tt, ep_done=ep_done):
                wt(sync, ep_done)
                inc(sync.dma_start(out=y[tt * 128:(tt + 1) * 128, :],
                                   in_=oep[:, tt % 2, :]), "sST", 16)
            blk.sync(p5_out)
            ep_rel[tt % 2] = ("sST", cnt["sST"])

        def fin(sync):
            sync.wait_ge(sems["sST"], cnt["sST"])
            sync.wait_ge(sems["sCC"], cnt["sCC"])
            if DEBUG_DUMP:
                for nm, srct in [("d_aqk_i", aqk_i), ("d_aqk_o", aqk_o), ("d_av_i", av_i),
                                 ("d_av_o", av_o), ("d_aav_i", aav_i), ("d_aav_o", aav_o),
                                 ("d_ffn16", ffn16d)]:
                    inc(sync.dma_start(out=dbg[nm], in_=srct.ap()), "sST", 16)
                sync.wait_ge(sems["sST"], cnt["sST"])
        blk.sync(fin)
        es4.close()

    return nc


_NC_CACHE = {}


def _host_prep(L0_attn_x, L0_attn_qkv_w, L0_attn_qkv_b, L0_attn_out_w,
               L0_attn_out_b, L0_ffn_weight, L0_ffn_w_scale, L0_ffn_bias):
    X = np.ascontiguousarray(np.asarray(L0_attn_x, np.float32).reshape(TOK, C))
    qkv_w = np.asarray(L0_attn_qkv_w, np.float32)
    qkv_b = np.asarray(L0_attn_qkv_b, np.float32)
    out_w = np.asarray(L0_attn_out_w, np.float32)
    out_b = np.asarray(L0_attn_out_b, np.float32)
    fw = np.asarray(L0_ffn_weight, np.int8)
    wsc = np.asarray(L0_ffn_w_scale, np.float32)[:, 0]
    fb = np.asarray(L0_ffn_bias, np.float32)

    rows, brows = [], []
    for r in range(W):
        for base in (0, 2048):
            for hh in (2 * r, 2 * r + 1):
                rows.append(qkv_w[base + hh * 128: base + (hh + 1) * 128])
                brows.append(qkv_b[base + hh * 128: base + (hh + 1) * 128])
    Wsel = np.concatenate(rows, axis=0)
    bsel = np.concatenate(brows, axis=0)
    wqk = np.ascontiguousarray(Wsel.T.reshape(CCH, 128, 32, 128).transpose(2, 1, 0, 3))
    bqk = np.ascontiguousarray(bsel.reshape(32, 128).T)

    Wv = qkv_w[4096:6144]
    wv = np.ascontiguousarray(Wv.T.reshape(CCH, 128, 4, 512).transpose(2, 1, 0, 3))
    vb_rep = np.ascontiguousarray(np.broadcast_to(qkv_b[4096:6144], (128, 2048)))

    wffn = np.ascontiguousarray(fw.T.reshape(CCH, 128, 2048))
    wsc_rep = np.ascontiguousarray(np.broadcast_to(wsc, (128, 2048)))
    fb_rep = np.ascontiguousarray(np.broadcast_to(fb, (128, 2048)))
    wout = np.ascontiguousarray(out_w.T.reshape(CCH, 128, 2048))
    outb_rep = np.ascontiguousarray(np.broadcast_to(out_b, (128, 2048)))

    tpos = np.arange(T, dtype=np.float64)
    freqs = np.arange(0, D, 2, dtype=np.float64) / D
    ang = tpos[:, None] * freqs[None, :]
    emb = np.concatenate([np.sin(ang), np.cos(ang)], axis=1).astype(np.float32)
    embT = np.ascontiguousarray(emb.T)

    jj = np.arange(128)[:, None]
    qq = np.arange(512)[None, :]
    mask = np.zeros((128, 4, 512), np.float32)
    for v in range(4):
        mask[:, v, :] = np.where(qq >= jj + v * 128, 0.0, NEGBIG)
    identf = np.eye(128, dtype=np.float32)
    identb = np.eye(128, dtype=np.float32).astype(ml_dtypes.bfloat16)
    onesb = np.ones((128, 128), np.float32).astype(ml_dtypes.bfloat16)

    kconst = np.zeros((128, 2), np.float32)
    kconst[:, 0] = 1e-5
    shared = dict(wqk=wqk, bqk=bqk, wv=wv, vb_rep=vb_rep, wffn=wffn, kconst=kconst,
                  wsc_rep=wsc_rep, fb_rep=fb_rep, wout=wout, outb_rep=outb_rep,
                  maskb=mask, identf=identf, identb=identb, onesb=onesb)
    in_maps = []
    for c in range(W):
        m = dict(shared)
        m["x_own"] = np.ascontiguousarray(X[c * TOWN:(c + 1) * TOWN])
        pos0 = (c % 2) * TOWN
        m["embt"] = np.ascontiguousarray(embT[:, pos0:pos0 + TOWN])
        in_maps.append(m)
    return in_maps


def kernel(L0_attn_x, L0_attn_qkv_w, L0_attn_qkv_b, L0_attn_out_w, L0_attn_out_b,
           L0_ffn_weight, L0_ffn_w_scale, L0_ffn_bias):
    in_maps = _host_prep(L0_attn_x, L0_attn_qkv_w, L0_attn_qkv_b, L0_attn_out_w,
                         L0_attn_out_b, L0_ffn_weight, L0_ffn_w_scale, L0_ffn_bias)
    if "nc" not in _NC_CACHE:
        _NC_CACHE["nc"] = build()
    nc = _NC_CACHE["nc"]
    res = run_bass_kernel_spmd(nc, in_maps, core_ids=list(range(W)))
    out = np.concatenate([np.asarray(r["y"]) for r in res.results], axis=0)
    return out.reshape(B, T, C)
